# version 03
# baseline (speedup 1.0000x reference)
import numpy as np
import contextlib
import concourse.bass as bass
import concourse.mybir as mybir
from concourse.bass_utils import run_bass_kernel_spmd

F32 = mybir.dt.float32
BF16 = mybir.dt.bfloat16
FP8 = mybir.dt.float8e4
AF = mybir.ActivationFunctionType
ALU = mybir.AluOpType

C = 2048
HC = 16
I = 5632
IC = 44
T = 2048
TT = 512
NTILE = 4
XSTR = 2056
MAGIC = 1.5 * (2.0 ** 23)
INV2PI = 1.0 / (2.0 * np.pi)
TWO_PI = 2.0 * np.pi

PHASES = ("ffn1", "attn", "ffn2")
DBG_FFN1 = False


class _Sched:

    ENGINES = ("tensor", "vector", "scalar", "gpsimd", "sync")

    def __init__(self):
        self.items = []

    def emit(self, engine, fn, after=(), dma=False, force=False, group=None):
        assert engine in self.ENGINES, engine
        if dma:
            assert group is not None, "every DMA needs an explicit group"
        sem = ("g:" + group) if dma else engine
        self.items.append(dict(engine=engine, after=[a for a in after if a is not None],
                               fn=fn, amt=16 if dma else 1, dma=dma, force=force, sem=sem))
        return len(self.items) - 1

    def build(self, nc, exit_engine="sync", exit_waits=()):
        needed = set(exit_waits)
        for i, it in enumerate(self.items):
            if it["force"]:
                needed.add(i)
            for a in it["after"]:
                src = self.items[a]
                if src["dma"] or src["engine"] != it["engine"]:
                    needed.add(a)
        val = {}
        counts = {}
        for i, it in enumerate(self.items):
            if i in needed:
                counts[it["sem"]] = counts.get(it["sem"], 0) + it["amt"]
                val[i] = counts[it["sem"]]
        used_sems = sorted(set(counts) | {exit_engine})
        used_eng = sorted({it["engine"] for it in self.items} | {exit_engine})
        stack = contextlib.ExitStack()
        sems = {nm: stack.enter_context(nc.semaphore(f"sem_{nm.replace(':', '_')}")) for nm in used_sems}
        block = stack.enter_context(nc.Block())

        def section(eng_name):
            def body(eng):
                last_wait = {}
                for i, it in enumerate(self.items):
                    if it["engine"] != eng_name:
                        continue
                    waits = {}
                    for a in it["after"]:
                        src = self.items[a]
                        if src["engine"] == eng_name and not src["dma"]:
                            continue
                        v = val.get(a)
                        if v is None:
                            raise RuntimeError(f"item {i} waits un-sem'd item {a}")
                        nm = src["sem"]
                        if v > waits.get(nm, 0):
                            waits[nm] = v
                    for nm, v in waits.items():
                        if v > last_wait.get(nm, 0):
                            eng.wait_ge(sems[nm], v)
                            last_wait[nm] = v
                    inst = it["fn"](eng)
                    if i in needed:
                        if inst is None:
                            raise RuntimeError(f"needed item {i} ({eng_name}) returned no instruction")
                        inst.then_inc(sems[it["sem"]], it["amt"])
                if eng_name == exit_engine:
                    for nm in used_sems:
                        c = counts.get(nm, 0)
                        if c > 0 and c > last_wait.get(nm, 0):
                            eng.wait_ge(sems[nm], c)
            return body

        for e in used_eng:
            getattr(block, e)(section(e))
        stack.close()


def build_program(phases=PHASES):
    nc = bass.Bass("TRN2", target_bir_lowering=False, debug=False)
    s = _Sched()
    gstack = contextlib.ExitStack()

    xT = nc.dram_tensor("xT", [C, T], F32, kind="ExternalInput").ap()
    waT = nc.dram_tensor("waT", [2 * C, C], F32, kind="ExternalInput").ap()
    wrT = nc.dram_tensor("wrT", [2 * C, C], F32, kind="ExternalInput").ap()
    wvT = nc.dram_tensor("wvT", [C, C], F32, kind="ExternalInput").ap()
    wgT = nc.dram_tensor("wgT", [C, I], F32, kind="ExternalInput").ap()
    wuT = nc.dram_tensor("wuT", [C, I], F32, kind="ExternalInput").ap()
    wdT = nc.dram_tensor("wdT", [I, C], F32, kind="ExternalInput").ap()
    ident = nc.dram_tensor("ident", [128, 128], F32, kind="ExternalInput").ap()
    outT = nc.dram_tensor("outT", [C, T], F32, kind="ExternalOutput").ap()

    resid = nc.dram_tensor("resid", [C, T], F32, kind="Internal").ap()
    wgq = nc.dram_tensor("wgq", [IC, 128, HC * 128], FP8, kind="Internal").ap()
    wuq = nc.dram_tensor("wuq", [IC, 128, HC * 128], FP8, kind="Internal").ap()
    wdq = nc.dram_tensor("wdq", [HC, 128, IC * 128], FP8, kind="Internal").ap()

    xT_r = xT.rearrange("(hc p) t -> p hc t", p=128)
    resid_r = resid.rearrange("(hc p) t -> p hc t", p=128)
    outT_r = outT.rearrange("(hc p) t -> p hc t", p=128)
    wgT_r = wgT.rearrange("(jc p) i -> p jc i", p=128)
    wuT_r = wuT.rearrange("(jc p) i -> p jc i", p=128)
    wdT_r = wdT.rearrange("(icb p) h -> p icb h", p=128)
    waT_r = waT.rearrange("(kc p) c -> p kc c", p=128)
    wrT_r = wrT.rearrange("(kc p) c -> p kc c", p=128)
    wvT_r = wvT.rearrange("(kc p) c -> p kc c", p=128)

    sb = lambda name, shape, dt: gstack.enter_context(nc.sbuf_tensor(name, shape, dt))
    idt = sb("idt", [128, 128], F32)
    ones1 = sb("ones1", [1, 128], F32)
    ones128 = sb("ones128", [128, 128], BF16)
    wsg = sb("wsg", [128, IC], F32)
    wsu = sb("wsu", [128, IC], F32)
    wsd = sb("wsd", [128, HC], F32)

    t_ident = s.emit("sync", lambda e: e.dma_start(out=idt[:], in_=ident[:, :]), dma=True, group="init")
    t_ones1 = s.emit("vector", lambda e: e.memset(ones1[:], 1.0))
    t_ones128 = s.emit("vector", lambda e: e.memset(ones128[:], 1.0))
    init_toks = [t_ident, t_ones1, t_ones128]

    qstack = contextlib.ExitStack()
    qsb = lambda name, shape, dt: qstack.enter_context(nc.sbuf_tensor(name, shape, dt))
    psq = qstack.enter_context(nc.psum_tensor("psq", [128, 128], F32))
    QH = IC // 4
    wraw = qsb("q_wraw", [128, HC * 128], F32)
    wdraw = qsb("q_wdraw", [128, QH * 128], F32)
    q8g = [qsb("q_q8g0", [128, HC * 128], FP8)] * 2
    q8d = qsb("q_q8d", [128, QH * 128], FP8)
    Mxq = qsb("q_Mx", [128, 128], F32)
    colsq = qsb("q_cols", [128, 2], F32)
    rowq = qsb("q_row", [1, 128], F32)

    cache_g = [None] * IC
    cache_u = [None] * IC
    cache_d = [None] * HC

    def q_scale_chain(t_reduce_src, after):
        tt = s.emit("tensor", lambda e: e.transpose(psq[:, 0:128], Mxq[:, 0:128], idt[:]),
                    after=after + [t_ident])
        tr = s.emit("vector", lambda e: e.tensor_reduce(out=colsq[:, 0:1], in_=psq[:, 0:128],
                                                        axis=mybir.AxisListType.X, op=ALU.max), after=[tt])
        t_t2 = s.emit("tensor", lambda e: e.transpose(psq[0:1, 0:128], colsq[:, 0:1], idt[:]),
                      after=[tr])
        t_cr = s.emit("vector", lambda e: e.tensor_copy(out=rowq[0:1, 0:128], in_=psq[0:1, 0:128]), after=[t_t2])
        t_ln = s.emit("scalar", lambda e: e.activation(out=rowq[0:1, 0:128], in_=rowq[0:1, 0:128],
                                                       func=AF.Ln), after=[t_cr])
        t_ex = s.emit("scalar", lambda e: e.activation(out=rowq[0:1, 0:128], in_=rowq[0:1, 0:128],
                                                       func=AF.Exp, scale=-1.0), after=[t_ln])
        t_um = s.emit("vector", lambda e: e.tensor_scalar(out=rowq[0:1, 0:128], in0=rowq[0:1, 0:128],
                                                          scalar1=224.0, scalar2=None, op0=ALU.mult), after=[t_ex])
        t_bc = s.emit("tensor", lambda e: e.matmul(psq[:, 0:128], ones1[0:1, :], rowq[0:1, 0:128],
                                                   start=True, stop=True), after=[t_um, t_ones1])
        return tr, t_bc

    prev_q = [None]
    prev_cw = [None, None]
    prev_ps = [None]

    def quant_gateup(which, ic, wT_r, cache, qbuf_par):
        qbuf_par = 0
        tl = s.emit("sync", lambda e: e.dma_start(out=wraw[:].rearrange("p (jc i) -> p jc i", jc=HC),
                                                  in_=wT_r[:, :, ic * 128:(ic + 1) * 128]),
                    after=[prev_q[0]] if prev_q[0] is not None else list(init_toks), dma=True, group="ql")
        v_in = wraw[:].rearrange("p (jc i) -> p i jc", jc=HC)
        tm = s.emit("vector", lambda e: e.tensor_reduce(out=Mxq[:, 0:128], in_=v_in, axis=mybir.AxisListType.X,
                                                        op=ALU.max, apply_absolute_value=True),
                    after=[tl, prev_ps[0]])
        tr, t_bc = q_scale_chain(tm, after=[tm])
        sc_t = wsg if which == "g" else wsu
        tsc = s.emit("vector", lambda e: e.tensor_scalar(out=sc_t[:, ic:ic + 1], in0=colsq[:, 0:1],
                                                         scalar1=1.0 / 224.0, scalar2=None, op0=ALU.mult), after=[tr])
        qb = q8g[qbuf_par]
        qts = []
        for jc in range(HC):
            qts.append(s.emit("vector", lambda e, jc=jc: e.tensor_tensor(
                out=qb[:, jc * 128:(jc + 1) * 128], in0=wraw[:, jc * 128:(jc + 1) * 128],
                in1=psq[:, 0:128], op=ALU.mult),
                after=[t_bc] + ([prev_cw[qbuf_par]] if jc == 0 else [qts[-1]])))
        t_cw = s.emit("gpsimd", lambda e, ic=ic: e.dma_start(out=cache[ic], in_=qb[:]), after=[qts[-1]], dma=True, group="qcg")
        prev_q[0] = qts[-1]
        prev_ps[0] = qts[-1]
        prev_cw[qbuf_par] = t_cw
        return t_cw

    prev_dq = [None]
    prev_dcw = [None]

    def quant_down2(hcc):
        t_prev = None
        for q in range(4):
            t_l = s.emit("sync", lambda e, q=q: e.dma_start(
                out=wdraw[:].rearrange("p (icb hh) -> p icb hh", icb=QH),
                in_=wdT_r[:, q * QH:(q + 1) * QH, hcc * 128:(hcc + 1) * 128]),
                after=[t_prev if t_prev is not None else (prev_dq[0] if prev_dq[0] is not None else prev_q[0])],
                dma=True, group="qd")
            v_in = wdraw[:].rearrange("p (icb hh) -> p hh icb", icb=QH)
            if q == 0:
                t_m = s.emit("vector", lambda e: e.tensor_reduce(out=Mxq[:, 0:128], in_=v_in,
                                                                 axis=mybir.AxisListType.X, op=ALU.max,
                                                                 apply_absolute_value=True),
                             after=[t_l, prev_ps[0]])
            else:
                t_r2 = s.emit("vector", lambda e: e.tensor_reduce(out=psq_scr[:, 0:128], in_=v_in,
                                                                  axis=mybir.AxisListType.X, op=ALU.max,
                                                                  apply_absolute_value=True), after=[t_l])
                t_m = s.emit("vector", lambda e: e.tensor_tensor(out=Mxq[:, 0:128], in0=Mxq[:, 0:128],
                                                                 in1=psq_scr[:, 0:128], op=ALU.max), after=[t_r2, t_m])
            t_prev = t_m
        tr, t_bc = q_scale_chain(t_prev, after=[t_prev])
        s.emit("vector", lambda e: e.tensor_scalar(out=wsd[:, hcc:hcc + 1], in0=colsq[:, 0:1],
                                                   scalar1=0.5 / 224.0, scalar2=None, op0=ALU.mult), after=[tr])
        cws = []
        t_data = t_prev
        for q in range(4):
            t_data = s.emit("sync", lambda e, q=q: e.dma_start(
                out=wdraw[:].rearrange("p (icb hh) -> p icb hh", icb=QH),
                in_=wdT_r[:, q * QH:(q + 1) * QH, hcc * 128:(hcc + 1) * 128]),
                after=[t_data, t_bc], dma=True, group="qd")
            qts = []
            for icb in range(QH):
                qts.append(s.emit("vector", lambda e, icb=icb: e.tensor_tensor(
                    out=q8d[:, icb * 128:(icb + 1) * 128], in0=wdraw[:, icb * 128:(icb + 1) * 128],
                    in1=psq[:, 0:128], op=ALU.mult),
                    after=[t_data, t_bc] + ([prev_dcw[0]] if icb == 0 else [qts[-1]])))
            t_cw = s.emit("gpsimd", lambda e, hcc=hcc, q=q: e.dma_start(
                out=wdq[hcc][:, q * QH * 128:(q + 1) * QH * 128], in_=q8d[:]), after=[qts[-1]], dma=True, group="qcd")
            prev_dcw[0] = t_cw
            cws.append(t_cw)
            t_data = qts[-1]
            prev_dq[0] = qts[-1]
            prev_ps[0] = qts[-1]
        return cws

    psq_scr = qsb("q_scr", [128, 128], F32)

    use_ffn = ("ffn1" in phases) or ("ffn2" in phases)
    if use_ffn:
        for ic in range(IC):
            cache_g[ic] = quant_gateup("g", ic, wgT_r, wgq, 0)
            cache_u[ic] = quant_gateup("u", ic, wuT_r, wuq, 1)
        for hcc in range(HC):
            cache_d[hcc] = quant_down2(hcc)

    def ffn(src_r, dst_r, prev_tail, tag):
        ph = contextlib.ExitStack()
        psb = lambda name, shape: ph.enter_context(nc.psum_tensor(name, shape, F32))
        ps_sum = psb(f"{tag}_ps_sum", [128, TT])
        ps_sq = psb(f"{tag}_ps_sq", [128, TT])
        ps_bc = psb(f"{tag}_ps_bc", [128, TT])
        ps_g = [psb(f"{tag}_ps_g0", [128, TT]), psb(f"{tag}_ps_g1", [128, TT])]
        ps_u = [psb(f"{tag}_ps_u0", [128, TT]), psb(f"{tag}_ps_u1", [128, TT])]
        ps_dn = [ps_sq, ps_sum]

        xin = ph.enter_context(nc.sbuf_tensor(f"{tag}_xin", [128, HC * TT], F32))
        xbb = [ph.enter_context(nc.sbuf_tensor(f"{tag}_xbb{k}", [128, TT], BF16)) for k in range(2)]
        qx = ph.enter_context(nc.sbuf_tensor(f"{tag}_qx", [128, HC * TT], FP8))
        t0 = ph.enter_context(nc.sbuf_tensor(f"{tag}_t0", [128, IC * TT], F32))
        qt = ph.enter_context(nc.sbuf_tensor(f"{tag}_qt", [128, IC * TT], FP8))
        scr1 = ph.enter_context(nc.sbuf_tensor(f"{tag}_scr1", [128, TT], F32))
        scr2 = ph.enter_context(nc.sbuf_tensor(f"{tag}_scr2", [128, TT], F32))
        Mx = ph.enter_context(nc.sbuf_tensor(f"{tag}_Mx", [128, TT], F32))
        cols = ph.enter_context(nc.sbuf_tensor(f"{tag}_cols", [128, 8], F32))
        rows = ph.enter_context(nc.sbuf_tensor(f"{tag}_rows", [1, 2 * TT], F32))
        bsc = ph.enter_context(nc.sbuf_tensor(f"{tag}_bsc", [128, TT], F32))
        qwg = [ph.enter_context(nc.sbuf_tensor(f"{tag}_qwg{k}", [128, HC * 128], FP8)) for k in range(2)]
        qwu = [ph.enter_context(nc.sbuf_tensor(f"{tag}_qwu{k}", [128, HC * 128], FP8)) for k in range(2)]
        qwd = [ph.enter_context(nc.sbuf_tensor(f"{tag}_qwd{k}", [128, IC * 128], FP8)) for k in range(2)]

        tail = list(prev_tail)

        def maxcol_bcast(after, extra_row_fn, consume_fn):
            nb = TT // 128
            toks = []
            for b in range(nb):
                tt = s.emit("tensor", lambda e, b=b: e.transpose(ps_bc[:, 0:128], Mx[:, b * 128:(b + 1) * 128], idt[:]),
                            after=after + [t_ident] + ([toks[-1]] if toks else []))
                rd = s.emit("vector", lambda e, b=b: e.tensor_reduce(out=cols[:, b:b + 1], in_=ps_bc[:, 0:128],
                                                                     axis=mybir.AxisListType.X, op=ALU.max),
                            after=[tt])
                toks.append(rd)
            row_toks = []
            for b in range(nb):
                t1 = s.emit("tensor", lambda e, b=b: e.transpose(ps_bc[0:1, 0:128], cols[:, b:b + 1], idt[:]),
                            after=[toks[-1]] + ([row_toks[-1]] if row_toks else []))
                c1 = s.emit("vector", lambda e, b=b: e.tensor_copy(out=rows[0:1, b * 128:(b + 1) * 128], in_=ps_bc[0:1, 0:128]),
                            after=[t1])
                row_toks.append(c1)
            t_ln = s.emit("scalar", lambda e: e.activation(out=rows[0:1, TT:2 * TT], in_=rows[0:1, 0:TT],
                                                           func=AF.Ln), after=row_toks)
            t_ex = s.emit("scalar", lambda e: e.activation(out=rows[0:1, TT:2 * TT], in_=rows[0:1, TT:2 * TT],
                                                           func=AF.Exp, scale=-1.0), after=[t_ln])
            t_uc = s.emit("vector", lambda e: e.tensor_scalar(out=rows[0:1, TT:2 * TT], in0=rows[0:1, TT:2 * TT],
                                                              scalar1=224.0, scalar2=None, op0=ALU.mult), after=[t_ex])
            tb = s.emit("tensor", lambda e: e.matmul(ps_bc[:, 0:TT], ones1[0:1, :], rows[0:1, TT:2 * TT], start=True, stop=True),
                        after=[t_uc, t_ones1])
            t_cons = consume_fn(tb)
            r_last = extra_row_fn([row_toks[-1], tb])
            tb2 = s.emit("tensor", lambda e: e.matmul(ps_bc[:, 0:TT], ones1[0:1, :], rows[0:1, TT:2 * TT], start=True, stop=True),
                         after=[r_last, t_cons])
            t_bsc = s.emit("vector", lambda e: e.tensor_copy(out=bsc[:, 0:TT], in_=ps_bc[:, 0:TT]), after=[tb2])
            return t_cons, t_bsc

        guard0 = prev_tail[-1] if prev_tail else None
        wg_rd = [guard0, guard0]
        wu_rd = [guard0, guard0]
        wd_rd = [guard0, guard0]
        eplg1 = [None, None]
        eplg3 = [None, None]
        d_ep1 = [None, None]
        st_prev = [None, None]

        for tile_i in range(NTILE):
            tk0 = tile_i * TT
            t_x = s.emit("sync", lambda e, tk0=tk0: e.dma_start(
                out=xin[:].rearrange("p (hc t) -> p hc t", hc=HC),
                in_=src_r[:, :, tk0:tk0 + TT]), after=tail, dma=True, group=tag + "x")
            mm_s = []
            mm_q = []
            for hc in range(HC):
                par = hc % 2
                tc = s.emit("scalar", lambda e, hc=hc, par=par: e.activation(
                    out=xbb[par][:], in_=xin[:, hc * TT:(hc + 1) * TT], func=AF.Copy),
                    after=[t_x] + ([mm_q[-2]] if len(mm_q) >= 2 else []))
                ms = s.emit("tensor", lambda e, hc=hc, par=par: e.matmul(
                    ps_sum[:, 0:TT], ones128[:], xbb[par][:], start=(hc == 0), stop=(hc == HC - 1)),
                    after=[tc, t_ones128])
                tq = s.emit("scalar", lambda e, hc=hc, par=par: e.activation(
                    out=xbb[par][:], in_=xin[:, hc * TT:(hc + 1) * TT], func=AF.Square),
                    after=[ms])
                mq = s.emit("tensor", lambda e, hc=hc, par=par: e.matmul(
                    ps_sq[:, 0:TT], ones128[:], xbb[par][:], start=(hc == 0), stop=(hc == HC - 1)),
                    after=[tq])
                mm_s.append(ms)
                mm_q.append(mq)
            t_sum = mm_s[-1]
            t_sqm = mm_q[-1]
            tree = []
            for hc in range(HC):
                td = s.emit("vector", lambda e, hc=hc: e.scalar_tensor_tensor(
                    out=scr1[:, 0:TT], in0=ps_sum[:, 0:TT], scalar=-1.0 / 2048.0,
                    in1=xin[:, hc * TT:(hc + 1) * TT], op0=ALU.mult, op1=ALU.add),
                    after=[t_sum] + ([tree[-1]] if tree else []))
                ta = s.emit("scalar", lambda e: e.activation(out=scr2[:, 0:TT], in_=scr1[:, 0:TT], func=AF.Abs),
                            after=[td])
                if hc == 0:
                    tm = s.emit("vector", lambda e: e.tensor_copy(out=Mx[:, 0:TT], in_=scr2[:, 0:TT]), after=[ta])
                else:
                    tm = s.emit("vector", lambda e: e.tensor_tensor(out=Mx[:, 0:TT], in0=Mx[:, 0:TT],
                                                                    in1=scr2[:, 0:TT], op=ALU.max), after=[ta])
                tree.append(tm)

            def xrow_fn(aft):
                r1 = s.emit("vector", lambda e: e.tensor_scalar(out=rows[0:1, TT:2 * TT], in0=ps_sum[0:1, 0:TT],
                                                                scalar1=1.0 / 2048.0, scalar2=None, op0=ALU.mult),
                            after=[t_sum] + aft)
                r2 = s.emit("vector", lambda e: e.tensor_tensor(out=rows[0:1, TT:2 * TT], in0=rows[0:1, TT:2 * TT],
                                                                in1=rows[0:1, TT:2 * TT], op=ALU.mult), after=[r1])
                r3 = s.emit("vector", lambda e: e.scalar_tensor_tensor(out=rows[0:1, TT:2 * TT], in0=ps_sq[0:1, 0:TT],
                                                                       scalar=1.0 / 2048.0, in1=rows[0:1, TT:2 * TT],
                                                                       op0=ALU.mult, op1=ALU.subtract),
                            after=[r2, t_sqm])
                r4 = s.emit("vector", lambda e: e.tensor_scalar(out=rows[0:1, TT:2 * TT], in0=rows[0:1, TT:2 * TT],
                                                                scalar1=1e-5, scalar2=None, op0=ALU.add), after=[r3])
                r5 = s.emit("scalar", lambda e: e.activation(out=rows[0:1, TT:2 * TT], in_=rows[0:1, TT:2 * TT], func=AF.Ln), after=[r4])
                r6 = s.emit("scalar", lambda e: e.activation(out=rows[0:1, TT:2 * TT], in_=rows[0:1, TT:2 * TT], func=AF.Exp,
                                                             scale=-0.5), after=[r5])
                r7 = s.emit("vector", lambda e: e.tensor_tensor(out=rows[0:1, TT:2 * TT], in0=rows[0:1, 0:TT],
                                                                in1=rows[0:1, TT:2 * TT], op=ALU.mult), after=[r6])
                r8 = s.emit("vector", lambda e: e.tensor_scalar(out=rows[0:1, TT:2 * TT], in0=rows[0:1, TT:2 * TT],
                                                                scalar1=1.0 / 224.0, scalar2=None, op0=ALU.mult), after=[r7])
                return r8

            def qx_consume(tb):
                toks = []
                for hc in range(HC):
                    td = s.emit("vector", lambda e, hc=hc: e.scalar_tensor_tensor(
                        out=scr1[:, 0:TT], in0=ps_sum[:, 0:TT], scalar=-1.0 / 2048.0,
                        in1=xin[:, hc * TT:(hc + 1) * TT], op0=ALU.mult, op1=ALU.add),
                        after=[tb] + ([toks[-1]] if toks else []))
                    tqx = s.emit("vector", lambda e, hc=hc: e.tensor_tensor(
                        out=qx[:, hc * TT:(hc + 1) * TT], in0=scr1[:, 0:TT], in1=ps_bc[:, 0:TT], op=ALU.mult),
                        after=[td])
                    toks.append(tqx)
                return toks[-1]

            t_qx, tb_bxs = maxcol_bcast(after=[tree[-1]], extra_row_fn=xrow_fn, consume_fn=qx_consume)

            tMt = None
            for ic in range(IC):
                par = ic % 2
                wg_tok = s.emit("sync", lambda e, ic=ic, par=par: e.dma_start(out=qwg[par][:], in_=wgq[ic]),
                                after=[cache_g[ic], wg_rd[par]], dma=True, group=f"{tag}g{par}")
                wu_tok = s.emit("sync", lambda e, ic=ic, par=par: e.dma_start(out=qwu[par][:], in_=wuq[ic]),
                                after=[cache_u[ic], wu_rd[par]], dma=True, group=f"{tag}u{par}")
                mmg = []
                for jc in range(HC):
                    mmg.append(s.emit("tensor", lambda e, jc=jc, par=par: e.matmul(
                        ps_g[par][:, 0:TT], qwg[par][:, jc * 128:(jc + 1) * 128], qx[:, jc * TT:(jc + 1) * TT],
                        start=(jc == 0), stop=(jc == HC - 1)),
                        after=[t_qx, wg_tok, eplg1[par]] + ([mmg[-1]] if mmg else [])))
                mmu = []
                for jc in range(HC):
                    mmu.append(s.emit("tensor", lambda e, jc=jc, par=par: e.matmul(
                        ps_u[par][:, 0:TT], qwu[par][:, jc * 128:(jc + 1) * 128], qx[:, jc * TT:(jc + 1) * TT],
                        start=(jc == 0), stop=(jc == HC - 1)),
                        after=[wu_tok, eplg3[par], mmg[-1]] + ([mmu[-1]] if mmu else [])))
                wg_rd[par] = mmg[-1]
                wu_rd[par] = mmu[-1]
                e1 = s.emit("vector", lambda e, ic=ic, par=par: e.scalar_tensor_tensor(
                    out=scr1[:, 0:TT], in0=ps_g[par][:, 0:TT], scalar=wsg[:, ic:ic + 1], in1=bsc[:, 0:TT],
                    op0=ALU.mult, op1=ALU.mult), after=[mmg[-1], tb_bxs])
                e2 = s.emit("scalar", lambda e: e.activation(out=scr1[:, 0:TT], in_=scr1[:, 0:TT], func=AF.Silu),
                            after=[e1])
                e3 = s.emit("vector", lambda e, ic=ic, par=par: e.scalar_tensor_tensor(
                    out=scr2[:, 0:TT], in0=ps_u[par][:, 0:TT], scalar=wsu[:, ic:ic + 1], in1=bsc[:, 0:TT],
                    op0=ALU.mult, op1=ALU.mult), after=[mmu[-1], e1])
                e4 = s.emit("vector", lambda e, ic=ic: e.tensor_tensor(
                    out=t0[:, ic * TT:(ic + 1) * TT], in0=scr1[:, 0:TT], in1=scr2[:, 0:TT], op=ALU.mult),
                    after=[e2, e3])
                eplg1[par] = e1
                eplg3[par] = e3
                ta = s.emit("scalar", lambda e, ic=ic: e.activation(out=scr2[:, 0:TT],
                                                                    in_=t0[:, ic * TT:(ic + 1) * TT], func=AF.Abs),
                            after=[e4])
                if ic == 0:
                    tMt = s.emit("vector", lambda e: e.tensor_copy(out=Mx[:, 0:TT], in_=scr2[:, 0:TT]),
                                 after=[ta, tb_bxs])
                else:
                    tMt = s.emit("vector", lambda e: e.tensor_tensor(out=Mx[:, 0:TT], in0=Mx[:, 0:TT],
                                                                     in1=scr2[:, 0:TT], op=ALU.max), after=[ta, tMt])

            def trow_fn(aft):
                r8 = s.emit("vector", lambda e: e.tensor_scalar(out=rows[0:1, TT:2 * TT], in0=rows[0:1, 0:TT],
                                                                scalar1=1.0 / 224.0, scalar2=None, op0=ALU.mult),
                            after=aft)
                return r8

            def qt_consume(tb):
                toks = []
                for ic in range(IC):
                    toks.append(s.emit("vector", lambda e, ic=ic: e.tensor_tensor(
                        out=qt[:, ic * TT:(ic + 1) * TT], in0=t0[:, ic * TT:(ic + 1) * TT],
                        in1=ps_bc[:, 0:TT], op=ALU.mult),
                        after=[tb] + ([toks[-1]] if toks else [])))
                return toks[-1]

            t_qt, tb_bts = maxcol_bcast(after=[tMt], extra_row_fn=trow_fn, consume_fn=qt_consume)

            st_toks = []
            for hcc in range(HC):
                par = hcc % 2
                wd_tok = s.emit("sync", lambda e, hcc=hcc, par=par: e.dma_start(out=qwd[par][:], in_=wdq[hcc]),
                                after=(cache_d[hcc] if isinstance(cache_d[hcc], list) else [cache_d[hcc]]) + [wd_rd[par]],
                                dma=True, group=f"{tag}d{par}")
                mmd = []
                for icb in range(IC):
                    mmd.append(s.emit("tensor", lambda e, icb=icb, par=par: e.matmul(
                        ps_dn[par][:, 0:TT], qwd[par][:, icb * 128:(icb + 1) * 128], qt[:, icb * TT:(icb + 1) * TT],
                        start=(icb == 0), stop=(icb == IC - 1)),
                        after=[t_qt, wd_tok, d_ep1[par]] + ([mmd[-1]] if mmd else [])))
                wd_rd[par] = mmd[-1]
                ep1 = s.emit("vector", lambda e, hcc=hcc, par=par: e.scalar_tensor_tensor(
                    out=scr1[:, 0:TT], in0=ps_dn[par][:, 0:TT], scalar=wsd[:, hcc:hcc + 1], in1=bsc[:, 0:TT],
                    op0=ALU.mult, op1=ALU.mult),
                    after=[mmd[-1], tb_bts])
                ep2 = s.emit("vector", lambda e, hcc=hcc: e.tensor_tensor(
                    out=xin[:, hcc * TT:(hcc + 1) * TT], in0=scr1[:, 0:TT],
                    in1=xin[:, hcc * TT:(hcc + 1) * TT], op=ALU.add),
                    after=[ep1])
                d_ep1[par] = ep1
                st = s.emit("scalar", lambda e, hcc=hcc, tk0=tk0: e.dma_start(
                    out=dst_r[:, hcc, tk0:tk0 + TT], in_=xin[:, hcc * TT:(hcc + 1) * TT]),
                    after=[ep2], dma=True, force=True, group=tag + "s")
                st_toks.append(st)
            tail = [st_toks[-1]]

        if DBG_FFN1 and tag == "f1":
            def dbg_out(name, ap, dt=F32, cast=False):
                od = nc.dram_tensor("dbg_" + name, list(ap.shape), dt, kind="ExternalOutput").ap()
                eng = "gpsimd" if cast else "sync"
                s.emit(eng, lambda e, od=od, ap=ap: e.dma_start(out=od, in_=ap),
                       after=list(tail), dma=True, force=True, group="dbg")
            dbg_out("wsg", wsg[:])
            dbg_out("wsu", wsu[:])
            dbg_out("wsd", wsd[:])
            dbg_out("rows", rows[0:1, :])
            dbg_out("bsc", bsc[:])
            dbg_out("Mx", Mx[:])
            dbg_out("qx", qx[:], cast=True)
            dbg_out("qwg0", qwg[0][:], cast=True)
            dbg_out("qwd0", qwd[0][:], cast=True)
            dbg_out("wgq0", wgq[0], cast=True)
            dbg_out("wgq40", wgq[40], cast=True)
            dbg_out("wdq0", wdq[0], cast=True)
            dbg_out("t0a", t0[:, 0:4 * TT])
            dbg_out("qta", qt[:, 0:4 * TT], cast=True)

        ph.close()
        return tail

    def attention(prev_tail):
        ph = contextlib.ExitStack()
        psb = lambda name, shape: ph.enter_context(nc.psum_tensor(name, shape, F32))
        ps_big = [psb("a_big0", [128, 2048]), psb("a_big1", [128, 2048])]
        xn = ph.enter_context(nc.sbuf_tensor("a_xn", [128, HC * XSTR], BF16))
        kv = ph.enter_context(nc.sbuf_tensor("a_kv", [128, HC * 2048], BF16))
        pool = [ph.enter_context(nc.sbuf_tensor(f"a_pl{k}", [128, 2048], F32)) for k in range(5)]
        xa = [ph.enter_context(nc.sbuf_tensor(f"a_xa{k}", [128, 512], F32)) for k in range(2)]
        xbb0 = ph.enter_context(nc.sbuf_tensor("a_xbb0", [128, 512], BF16))
        xbb = [xbb0, xbb0]
        scr = ph.enter_context(nc.sbuf_tensor("a_scr", [128, 512], F32))
        rsc = ph.enter_context(nc.sbuf_tensor("a_rsc", [1, 1024], F32))
        binv = ph.enter_context(nc.sbuf_tensor("a_binv", [128, 512], F32))
        yst = [scr, binv]
        wa = [ph.enter_context(nc.sbuf_tensor(f"a_wa{k}", [128, 2 * HC * 128], BF16)) for k in range(2)]
        wv = [ph.enter_context(nc.sbuf_tensor(f"a_wv{k}", [128, HC * 128], BF16)) for k in range(2)]

        tail = list(prev_tail)

        t_z = None
        for hc in range(HC):
            t_z = s.emit("vector", lambda e, hc=hc: e.memset(xn[:, hc * XSTR:hc * XSTR + 1], 0.0), after=tail)

        ln_prev = None
        xa_rd = [None, None]
        for tb in range(4):
            tk0 = tb * 512
            mm_s = []
            mm_q = []
            for hc in range(HC):
                par = hc % 2
                t_l = s.emit("sync", lambda e, hc=hc, par=par, tk0=tk0: e.dma_start(
                    out=xa[par][:], in_=resid_r[:, hc, tk0:tk0 + 512]),
                    after=([xa_rd[par]] if xa_rd[par] is not None else tail) + ([ln_prev] if (hc < 2 and ln_prev) else []),
                    dma=True, group=f"ax{par}")
                tc = s.emit("scalar", lambda e, par=par: e.activation(out=xbb[par][:], in_=xa[par][:], func=AF.Copy),
                            after=[t_l] + ([mm_q[-1]] if mm_q else []))
                ms = s.emit("tensor", lambda e, hc=hc, par=par: e.matmul(
                    ps_big[0][:, 0:512], ones128[:], xbb[par][:], start=(hc == 0), stop=(hc == HC - 1)),
                    after=[tc, t_ones128] + ([ln_prev] if (hc == 0 and ln_prev) else []))
                tq = s.emit("scalar", lambda e, par=par: e.activation(out=xbb[par][:], in_=xa[par][:], func=AF.Square),
                            after=[ms])
                mq = s.emit("tensor", lambda e, hc=hc, par=par: e.matmul(
                    ps_big[0][:, 512:1024], ones128[:], xbb[par][:], start=(hc == 0), stop=(hc == HC - 1)),
                    after=[tq])
                mm_s.append(ms)
                mm_q.append(mq)
                xa_rd[par] = tq
            r1 = s.emit("vector", lambda e: e.tensor_scalar(out=rsc[0:1, 0:512], in0=ps_big[0][0:1, 0:512],
                                                            scalar1=1.0 / 2048.0, scalar2=None, op0=ALU.mult),
                        after=[mm_s[-1]])
            r2 = s.emit("vector", lambda e: e.tensor_tensor(out=rsc[0:1, 512:1024], in0=rsc[0:1, 0:512],
                                                            in1=rsc[0:1, 0:512], op=ALU.mult), after=[r1])
            r3 = s.emit("vector", lambda e: e.scalar_tensor_tensor(out=rsc[0:1, 512:1024], in0=ps_big[0][0:1, 512:1024],
                                                                   scalar=1.0 / 2048.0, in1=rsc[0:1, 512:1024],
                                                                   op0=ALU.mult, op1=ALU.subtract), after=[r2, mm_q[-1]])
            r4 = s.emit("vector", lambda e: e.tensor_scalar(out=rsc[0:1, 512:1024], in0=rsc[0:1, 512:1024], scalar1=1e-5,
                                                            scalar2=None, op0=ALU.add), after=[r3])
            r5 = s.emit("scalar", lambda e: e.activation(out=rsc[0:1, 512:1024], in_=rsc[0:1, 512:1024], func=AF.Ln), after=[r4])
            r6 = s.emit("scalar", lambda e: e.activation(out=rsc[0:1, 512:1024], in_=rsc[0:1, 512:1024], func=AF.Exp,
                                                         scale=-0.5), after=[r5])
            tb1 = s.emit("tensor", lambda e: e.matmul(ps_big[1][:, 0:512], ones1[0:1, :], rsc[0:1, 512:1024],
                                                      start=True, stop=True),
                         after=[r6, t_ones1] + ([ln_prev] if ln_prev else []))
            tb2 = s.emit("vector", lambda e: e.tensor_copy(out=binv[:, 0:512], in_=ps_big[1][:, 0:512]), after=[tb1])
            for hc in range(HC):
                par = hc % 2
                t_l2 = s.emit("sync", lambda e, hc=hc, par=par, tk0=tk0: e.dma_start(
                    out=xa[par][:], in_=resid_r[:, hc, tk0:tk0 + 512]),
                    after=[xa_rd[par]], dma=True, group=f"ax{par}")
                std = s.emit("vector", lambda e, par=par: e.scalar_tensor_tensor(
                    out=scr[:, 0:512], in0=ps_big[0][:, 0:512], scalar=-1.0 / 2048.0,
                    in1=xa[par][:], op0=ALU.mult, op1=ALU.add), after=[t_l2, tb2])
                tw = s.emit("vector", lambda e, hc=hc, tb=tb: e.tensor_tensor(
                    out=xn[:, hc * XSTR + 1 + tb * 512: hc * XSTR + 1 + tb * 512 + 512],
                    in0=scr[:, 0:512], in1=binv[:, 0:512], op=ALU.mult), after=[std, t_z])
                xa_rd[par] = std
                ln_prev = tw
        t_ln = ln_prev

        kv_toks = {}
        pair_planes = {}
        wa_ld = [None, None]
        wa_rd = [None, None]
        evac_done = [None, None]
        scan_chain = []
        oc_order = []
        for j in range(8):
            oc_order += [j, j + 8]
        for idx, oc in enumerate(oc_order):
            par = idx % 2
            t_w = s.emit("gpsimd", lambda e, oc=oc, par=par: e.dma_start(
                out=wa[par][:].rearrange("p (kc c) -> p kc c", kc=2 * HC),
                in_=waT_r[:, :, oc * 128:(oc + 1) * 128]),
                after=[wa_ld[par], wa_rd[par]] + ([] if wa_ld[par] else tail), dma=True, group=f"awa{par}")
            wa_ld[par] = t_w
            mm = []
            for tb in range(4):
                for kc in range(2 * HC):
                    hcs = kc if kc < HC else kc - HC
                    off = 0 if kc < HC else 1
                    mm.append(s.emit("tensor", lambda e, kc=kc, tb=tb, par=par, hcs=hcs, off=off: e.matmul(
                        ps_big[par][:, tb * 512:(tb + 1) * 512],
                        wa[par][:, kc * 128:(kc + 1) * 128],
                        xn[:, hcs * XSTR + off + tb * 512: hcs * XSTR + off + tb * 512 + 512],
                        start=(kc == 0), stop=(kc == 2 * HC - 1)),
                        after=[t_w, t_ln, evac_done[par]] + ([mm[-1]] if mm else [])))
            wa_rd[par] = mm[-1]
            plane = pool[0] if oc < 8 else pool[1]
            ev1 = s.emit("scalar", lambda e, par=par, plane=plane: e.activation(
                out=plane[:], in_=ps_big[par][:], func=AF.Sigmoid),
                after=[mm[-1]] + scan_chain[-1:])
            ev2 = s.emit("vector", lambda e, plane=plane: e.tensor_scalar(
                out=plane[:], in0=plane[:], scalar1=0.5, scalar2=None, op0=ALU.max), after=[ev1])
            evac_done[par] = ev2
            pair_planes[oc] = ev2
            if oc >= 8:
                jj = oc - 8
                A, B, Cp, D, Fm = pool[0], pool[1], pool[2], pool[3], pool[4]
                dep = [pair_planes[jj], pair_planes[oc]] + scan_chain[-1:]
                t1 = s.emit("vector", lambda e: e.tensor_tensor(out=Fm[:], in0=B[:], in1=A[:], op=ALU.is_gt), after=dep)
                t2 = s.emit("scalar", lambda e: e.activation(out=Cp[:], in_=A[:], func=AF.Square), after=dep)
                t3 = s.emit("scalar", lambda e: e.activation(out=D[:], in_=B[:], func=AF.Square), after=[t2])
                t4 = s.emit("vector", lambda e: e.tensor_tensor(out=Cp[:], in0=Cp[:], in1=D[:], op=ALU.add), after=[t1, t3])
                t5 = s.emit("scalar", lambda e: e.activation(out=Cp[:], in_=Cp[:], func=AF.Ln), after=[t4])
                t6 = s.emit("vector", lambda e: e.tensor_tensor_scan(out=D[:], data0=Cp[:], data1=Cp[:],
                                                                     initial=-1e30, op0=ALU.max, op1=ALU.bypass), after=[t5])
                t7 = s.emit("vector", lambda e: e.tensor_tensor(out=D[:], in0=Cp[:], in1=D[:], op=ALU.subtract), after=[t6])
                t8 = s.emit("vector", lambda e: e.tensor_tensor_scan(out=D[:], data0=D[:], data1=D[:],
                                                                     initial=0.0, op0=ALU.add, op1=ALU.bypass), after=[t7])
                t9 = s.emit("scalar", lambda e: e.activation(out=D[:], in_=D[:], func=AF.Exp, scale=0.5), after=[t8])
                t10 = s.emit("vector", lambda e: e.tensor_tensor(out=Cp[:], in0=A[:], in1=B[:], op=ALU.max), after=[t7])
                t11 = s.emit("scalar", lambda e: e.activation(out=Cp[:], in_=Cp[:], func=AF.Ln), after=[t10, t9])
                t12 = s.emit("scalar", lambda e: e.activation(out=Cp[:], in_=Cp[:], func=AF.Exp, scale=-1.0), after=[t11])
                t13 = s.emit("vector", lambda e: e.tensor_tensor(out=B[:], in0=A[:], in1=B[:], op=ALU.min), after=[t10])
                t14 = s.emit("vector", lambda e: e.tensor_tensor(out=Cp[:], in0=B[:], in1=Cp[:], op=ALU.mult), after=[t12, t13])
                t15 = s.emit("scalar", lambda e: e.activation(out=Cp[:], in_=Cp[:], func=AF.Arctan), after=[t14])
                u1 = s.emit("vector", lambda e: e.tensor_tensor(out=A[:], in0=Fm[:], in1=Cp[:], op=ALU.mult), after=[t15])
                u2 = s.emit("vector", lambda e: e.scalar_tensor_tensor(out=A[:], in0=A[:], scalar=-2.0,
                                                                       in1=Cp[:], op0=ALU.mult, op1=ALU.add), after=[u1])
                u3 = s.emit("vector", lambda e: e.scalar_tensor_tensor(out=A[:], in0=Fm[:], scalar=float(np.pi / 2),
                                                                       in1=A[:], op0=ALU.mult, op1=ALU.add), after=[u2])
                t19 = s.emit("vector", lambda e: e.tensor_tensor_scan(out=A[:], data0=A[:], data1=A[:],
                                                                      initial=0.0, op0=ALU.add, op1=ALU.bypass), after=[u3])
                c1 = s.emit("vector", lambda e: e.tensor_scalar(out=Cp[:], in0=A[:], scalar1=INV2PI, scalar2=0.25,
                                                                op0=ALU.mult, op1=ALU.add), after=[t19])
                c2 = s.emit("vector", lambda e: e.tensor_scalar(out=B[:], in0=Cp[:], scalar1=MAGIC, scalar2=None,
                                                                op0=ALU.add), after=[c1])
                c3 = s.emit("vector", lambda e: e.tensor_scalar(out=B[:], in0=B[:], scalar1=MAGIC, scalar2=None,
                                                                op0=ALU.subtract), after=[c2])
                c4 = s.emit("vector", lambda e: e.tensor_tensor(out=Cp[:], in0=Cp[:], in1=B[:], op=ALU.subtract), after=[c3])
                c5 = s.emit("scalar", lambda e: e.activation(out=Cp[:], in_=Cp[:], func=AF.Sin, scale=TWO_PI), after=[c4])
                s1 = s.emit("vector", lambda e: e.tensor_scalar(out=B[:], in0=A[:], scalar1=INV2PI, scalar2=MAGIC,
                                                                op0=ALU.mult, op1=ALU.add), after=[c4])
                s2 = s.emit("vector", lambda e: e.tensor_scalar(out=B[:], in0=B[:], scalar1=MAGIC, scalar2=None,
                                                                op0=ALU.subtract), after=[s1])
                s3 = s.emit("vector", lambda e: e.scalar_tensor_tensor(out=B[:], in0=A[:], scalar=INV2PI,
                                                                       in1=B[:], op0=ALU.mult, op1=ALU.subtract), after=[s2])
                s4 = s.emit("scalar", lambda e: e.activation(out=B[:], in_=B[:], func=AF.Sin, scale=TWO_PI), after=[s3, c5])
                k1 = s.emit("vector", lambda e, jj=jj: e.tensor_tensor(out=kv[:, jj * 2048:(jj + 1) * 2048],
                                                                       in0=D[:], in1=Cp[:], op=ALU.mult), after=[s4, t9])
                k2 = s.emit("vector", lambda e, jj=jj: e.tensor_tensor(out=kv[:, (jj + 8) * 2048:(jj + 9) * 2048],
                                                                       in0=D[:], in1=B[:], op=ALU.mult), after=[k1])
                kv_toks[jj] = k2
                scan_chain.append(k2)

        t_kv = scan_chain[-1]

        wv_ld = [None, None]
        wr_ld = [None, None]
        val_rd = [t_kv, t_kv]
        rec_rd = [t_kv, t_kv]
        ev_done = [None, None]
        st_toks = []
        xa2_rd = [None, None]
        for oc in range(HC):
            par = oc % 2
            t_wv = s.emit("gpsimd", lambda e, oc=oc, par=par: e.dma_start(
                out=wv[par][:].rearrange("p (kc c) -> p kc c", kc=HC),
                in_=wvT_r[:, :, oc * 128:(oc + 1) * 128]),
                after=[wv_ld[par] or t_kv, val_rd[par]], dma=True, group=f"awv{par}")
            wv_ld[par] = t_wv
            mmv = []
            for tb in range(4):
                for kc in range(HC):
                    mmv.append(s.emit("tensor", lambda e, kc=kc, tb=tb, par=par: e.matmul(
                        ps_big[par][:, tb * 512:(tb + 1) * 512],
                        wv[par][:, kc * 128:(kc + 1) * 128],
                        kv[:, kc * 2048 + tb * 512: kc * 2048 + tb * 512 + 512],
                        start=(kc == 0), stop=(kc == HC - 1)),
                        after=[t_wv, t_kv, ev_done[par]] + ([mmv[-1]] if mmv else [])))
            val_rd[par] = mmv[-1]
            t_wr = s.emit("gpsimd", lambda e, oc=oc, par=par: e.dma_start(
                out=wa[par][:].rearrange("p (kc c) -> p kc c", kc=2 * HC),
                in_=wrT_r[:, :, oc * 128:(oc + 1) * 128]),
                after=[wr_ld[par] or t_kv, rec_rd[par] if wr_ld[par] else wa_rd[par]], dma=True, group=f"awa{par}")
            wr_ld[par] = t_wr
            evv = s.emit("vector", lambda e, par=par: e.tensor_copy(out=pool[3][:], in_=ps_big[par][:]),
                         after=[mmv[-1]])
            mmr = []
            for tb in range(4):
                for kc in range(2 * HC):
                    hcs = kc if kc < HC else kc - HC
                    off = 0 if kc < HC else 1
                    mmr.append(s.emit("tensor", lambda e, kc=kc, tb=tb, par=par, hcs=hcs, off=off: e.matmul(
                        ps_big[par][:, tb * 512:(tb + 1) * 512],
                        wa[par][:, kc * 128:(kc + 1) * 128],
                        xn[:, hcs * XSTR + off + tb * 512: hcs * XSTR + off + tb * 512 + 512],
                        start=(kc == 0), stop=(kc == 2 * HC - 1)),
                        after=[t_wr, evv] + ([mmr[-1]] if mmr else [])))
            rec_rd[par] = mmr[-1]
            er1 = s.emit("scalar", lambda e, par=par: e.activation(out=pool[4][:], in_=ps_big[par][:], func=AF.Sigmoid),
                         after=[mmr[-1]])
            er2 = s.emit("vector", lambda e: e.tensor_scalar(out=pool[4][:], in0=pool[4][:], scalar1=0.5,
                                                             scalar2=None, op0=ALU.max), after=[er1])
            ev_done[par] = er2
            er3 = s.emit("vector", lambda e: e.tensor_tensor(out=pool[4][:], in0=pool[4][:], in1=pool[4][:],
                                                             op=ALU.mult), after=[er2])
            er4 = s.emit("vector", lambda e: e.tensor_tensor(out=pool[4][:], in0=pool[4][:], in1=pool[4][:],
                                                             op=ALU.mult), after=[er3])
            er5 = s.emit("vector", lambda e: e.tensor_tensor(out=pool[3][:], in0=pool[3][:], in1=pool[4][:],
                                                             op=ALU.mult), after=[er4])
            for tb in range(4):
                sp = tb % 2
                t_x1 = s.emit("sync", lambda e, oc=oc, tb=tb, sp=sp: e.dma_start(
                    out=xa[sp][:], in_=resid_r[:, oc, tb * 512:(tb + 1) * 512]),
                    after=[er5, xa2_rd[sp] if xa2_rd[sp] is not None else xa_rd[sp]], dma=True, group=f"ax{sp}")
                ta = s.emit("vector", lambda e, tb=tb, sp=sp: e.tensor_tensor(
                    out=yst[sp][:], in0=pool[3][:, tb * 512:(tb + 1) * 512], in1=xa[sp][:], op=ALU.add),
                    after=[t_x1] + ([st_toks[-2]] if len(st_toks) >= 2 else []))
                st = s.emit("scalar", lambda e, oc=oc, tb=tb, sp=sp: e.dma_start(
                    out=resid_r[:, oc, tb * 512:(tb + 1) * 512], in_=yst[sp][:]), after=[ta], dma=True, force=True, group="as")
                xa2_rd[sp] = ta
                st_toks.append(st)
        ph.close()
        return [st_toks[-1]]

    tail = list(init_toks)
    if "ffn1" in phases:
        tail = ffn(xT_r, resid_r if ("attn" in phases or "ffn2" in phases) else outT_r,
                   prev_tail=tail, tag="f1")
    qstack.close()
    if "attn" in phases:
        tail = attention(tail)
    if "ffn2" in phases:
        tail = ffn(resid_r, outT_r, prev_tail=tail, tag="f2")

    s.build(nc, exit_engine="sync", exit_waits=tuple(t for t in tail if t is not None))
    gstack.close()
    return nc


_PERM = np.concatenate([np.arange(0, C, 2), np.arange(1, C, 2)])


def _prep_weights(L0_attn_W_act, L0_attn_W_val, L0_attn_W_rec, L0_ffn_w_gate,
                  L0_ffn_w_up, L0_ffn_w_down, L0_ffn_down_reorder_index):
    wa = np.ascontiguousarray(np.asarray(L0_attn_W_act, dtype=np.float32)[_PERM, :].T)
    wr = np.ascontiguousarray(np.asarray(L0_attn_W_rec, dtype=np.float32).T)
    wv = np.ascontiguousarray(np.asarray(L0_attn_W_val, dtype=np.float32)[:, _PERM].T)
    wg = np.ascontiguousarray(np.asarray(L0_ffn_w_gate, dtype=np.float32).T)
    wu = np.ascontiguousarray(np.asarray(L0_ffn_w_up, dtype=np.float32).T)
    wd = np.asarray(L0_ffn_w_down, dtype=np.float32)
    idx = np.asarray(L0_ffn_down_reorder_index)
    wde = np.empty_like(wd)
    wde[:, idx] = wd
    wdT = np.ascontiguousarray(wde.T)
    return wa, wr, wv, wg, wu, wdT


def kernel(L0_attn_x, L0_attn_W_act, L0_attn_W_val, L0_attn_W_rec,
           L0_ffn_w_gate, L0_ffn_w_up, L0_ffn_w_down,
           L0_ffn_up_reorder_index, L0_ffn_down_reorder_index):
    x = np.asarray(L0_attn_x, dtype=np.float32)
    B = x.shape[0]
    wa, wr, wv, wg, wu, wdT = _prep_weights(L0_attn_W_act, L0_attn_W_val, L0_attn_W_rec,
                                            L0_ffn_w_gate, L0_ffn_w_up, L0_ffn_w_down,
                                            L0_ffn_down_reorder_index)
    ident = np.eye(128, dtype=np.float32)
    nc = build_program(PHASES)
    in_maps = []
    for b in range(B):
        in_maps.append({
            "xT": np.ascontiguousarray(x[b].T),
            "waT": wa, "wrT": wr, "wvT": wv, "wgT": wg, "wuT": wu, "wdT": wdT,
            "ident": ident,
        })
    res = run_bass_kernel_spmd(nc, in_maps, core_ids=list(range(B)))
    out = np.stack([np.ascontiguousarray(r["outT"].T) for r in res.results], axis=0)
    return out


# version 04
# speedup vs baseline: 1.2107x; 1.2107x over previous
import numpy as np
import contextlib
import concourse.bass as bass
import concourse.mybir as mybir
from concourse.bass_utils import run_bass_kernel_spmd

F32 = mybir.dt.float32
BF16 = mybir.dt.bfloat16
FP8 = mybir.dt.float8e4
AF = mybir.ActivationFunctionType
ALU = mybir.AluOpType

C = 2048
HC = 16
I = 5632
IC = 44
T = 2048
TT = 512
NTILE = 4
XSTR = 2056
MAGIC = 1.5 * (2.0 ** 23)
INV2PI = 1.0 / (2.0 * np.pi)
TWO_PI = 2.0 * np.pi

PHASES = ("ffn1", "attn", "ffn2")
DBG_FFN1 = False


class _Sched:

    ENGINES = ("tensor", "vector", "scalar", "gpsimd", "sync")

    def __init__(self):
        self.items = []

    def emit(self, engine, fn, after=(), dma=False, force=False, group=None):
        assert engine in self.ENGINES, engine
        if dma:
            assert group is not None, "every DMA needs an explicit group"
        sem = ("g:" + group) if dma else engine
        self.items.append(dict(engine=engine, after=[a for a in after if a is not None],
                               fn=fn, amt=16 if dma else 1, dma=dma, force=force, sem=sem))
        return len(self.items) - 1

    def build(self, nc, exit_engine="sync", exit_waits=()):
        needed = set(exit_waits)
        for i, it in enumerate(self.items):
            if it["force"]:
                needed.add(i)
            for a in it["after"]:
                src = self.items[a]
                if src["dma"] or src["engine"] != it["engine"]:
                    needed.add(a)
        val = {}
        counts = {}
        for i, it in enumerate(self.items):
            if i in needed:
                counts[it["sem"]] = counts.get(it["sem"], 0) + it["amt"]
                val[i] = counts[it["sem"]]
        used_sems = sorted(set(counts) | {exit_engine})
        used_eng = sorted({it["engine"] for it in self.items} | {exit_engine})
        stack = contextlib.ExitStack()
        sems = {nm: stack.enter_context(nc.semaphore(f"sem_{nm.replace(':', '_')}")) for nm in used_sems}
        block = stack.enter_context(nc.Block())

        def section(eng_name):
            def body(eng):
                last_wait = {}
                for i, it in enumerate(self.items):
                    if it["engine"] != eng_name:
                        continue
                    waits = {}
                    for a in it["after"]:
                        src = self.items[a]
                        if src["engine"] == eng_name and not src["dma"]:
                            continue
                        v = val.get(a)
                        if v is None:
                            raise RuntimeError(f"item {i} waits un-sem'd item {a}")
                        nm = src["sem"]
                        if v > waits.get(nm, 0):
                            waits[nm] = v
                    for nm, v in waits.items():
                        if v > last_wait.get(nm, 0):
                            eng.wait_ge(sems[nm], v)
                            last_wait[nm] = v
                    inst = it["fn"](eng)
                    if i in needed:
                        if inst is None:
                            raise RuntimeError(f"needed item {i} ({eng_name}) returned no instruction")
                        inst.then_inc(sems[it["sem"]], it["amt"])
                if eng_name == exit_engine:
                    for nm in used_sems:
                        c = counts.get(nm, 0)
                        if c > 0 and c > last_wait.get(nm, 0):
                            eng.wait_ge(sems[nm], c)
            return body

        for e in used_eng:
            getattr(block, e)(section(e))
        stack.close()


def build_program(phases=PHASES):
    nc = bass.Bass("TRN2", target_bir_lowering=False, debug=False)
    s = _Sched()
    gstack = contextlib.ExitStack()

    xT = nc.dram_tensor("xT", [C, T], F32, kind="ExternalInput").ap()
    waT = nc.dram_tensor("waT", [2 * C, C], F32, kind="ExternalInput").ap()
    wrT = nc.dram_tensor("wrT", [2 * C, C], F32, kind="ExternalInput").ap()
    wvT = nc.dram_tensor("wvT", [C, C], F32, kind="ExternalInput").ap()
    wgT = nc.dram_tensor("wgT", [C, I], F32, kind="ExternalInput").ap()
    wuT = nc.dram_tensor("wuT", [C, I], F32, kind="ExternalInput").ap()
    wdT = nc.dram_tensor("wdT", [I, C], F32, kind="ExternalInput").ap()
    ident = nc.dram_tensor("ident", [128, 128], F32, kind="ExternalInput").ap()
    outT = nc.dram_tensor("outT", [C, T], F32, kind="ExternalOutput").ap()

    resid = nc.dram_tensor("resid", [C, T], F32, kind="Internal").ap()
    wgq = nc.dram_tensor("wgq", [IC, 128, HC * 128], FP8, kind="Internal").ap()
    wuq = nc.dram_tensor("wuq", [IC, 128, HC * 128], FP8, kind="Internal").ap()
    wdq = nc.dram_tensor("wdq", [HC, 128, IC * 128], FP8, kind="Internal").ap()

    xT_r = xT.rearrange("(hc p) t -> p hc t", p=128)
    resid_r = resid.rearrange("(hc p) t -> p hc t", p=128)
    outT_r = outT.rearrange("(hc p) t -> p hc t", p=128)
    wgT_r = wgT.rearrange("(jc p) i -> p jc i", p=128)
    wuT_r = wuT.rearrange("(jc p) i -> p jc i", p=128)
    wdT_r = wdT.rearrange("(icb p) h -> p icb h", p=128)
    waT_r = waT.rearrange("(kc p) c -> p kc c", p=128)
    wrT_r = wrT.rearrange("(kc p) c -> p kc c", p=128)
    wvT_r = wvT.rearrange("(kc p) c -> p kc c", p=128)

    sb = lambda name, shape, dt: gstack.enter_context(nc.sbuf_tensor(name, shape, dt))
    idt = sb("idt", [128, 128], F32)
    ones1 = sb("ones1", [1, 128], F32)
    ones128 = sb("ones128", [128, 128], BF16)
    wsg = sb("wsg", [128, IC], F32)
    wsu = sb("wsu", [128, IC], F32)
    wsd = sb("wsd", [128, HC], F32)

    t_ident = s.emit("sync", lambda e: e.dma_start(out=idt[:], in_=ident[:, :]), dma=True, group="init")
    t_ones1 = s.emit("vector", lambda e: e.memset(ones1[:], 1.0))
    t_ones128 = s.emit("vector", lambda e: e.memset(ones128[:], 1.0))
    init_toks = [t_ident, t_ones1, t_ones128]

    qstack = contextlib.ExitStack()
    qsb = lambda name, shape, dt: qstack.enter_context(nc.sbuf_tensor(name, shape, dt))
    psq = qstack.enter_context(nc.psum_tensor("psq", [128, 128], F32))
    QH = IC // 4
    wraw = qsb("q_wraw", [128, HC * 128], F32)
    wdraw = qsb("q_wdraw", [128, QH * 128], F32)
    q8g = [qsb("q_q8g0", [128, HC * 128], FP8)] * 2
    q8d = qsb("q_q8d", [128, QH * 128], FP8)
    Mxq = qsb("q_Mx", [128, 128], F32)
    colsq = qsb("q_cols", [128, 2], F32)
    rowq = qsb("q_row", [1, 128], F32)

    cache_g = [None] * IC
    cache_u = [None] * IC
    cache_d = [None] * HC

    def q_scale_chain(t_reduce_src, after):
        tt = s.emit("tensor", lambda e: e.transpose(psq[:, 0:128], Mxq[:, 0:128], idt[:]),
                    after=after + [t_ident])
        tr = s.emit("vector", lambda e: e.tensor_reduce(out=colsq[:, 0:1], in_=psq[:, 0:128],
                                                        axis=mybir.AxisListType.X, op=ALU.max), after=[tt])
        t_t2 = s.emit("tensor", lambda e: e.transpose(psq[0:1, 0:128], colsq[:, 0:1], idt[:]),
                      after=[tr])
        t_cr = s.emit("vector", lambda e: e.tensor_copy(out=rowq[0:1, 0:128], in_=psq[0:1, 0:128]), after=[t_t2])
        t_ln = s.emit("scalar", lambda e: e.activation(out=rowq[0:1, 0:128], in_=rowq[0:1, 0:128],
                                                       func=AF.Ln), after=[t_cr])
        t_ex = s.emit("scalar", lambda e: e.activation(out=rowq[0:1, 0:128], in_=rowq[0:1, 0:128],
                                                       func=AF.Exp, scale=-1.0), after=[t_ln])
        t_um = s.emit("vector", lambda e: e.tensor_scalar(out=rowq[0:1, 0:128], in0=rowq[0:1, 0:128],
                                                          scalar1=224.0, scalar2=None, op0=ALU.mult), after=[t_ex])
        t_bc = s.emit("tensor", lambda e: e.matmul(psq[:, 0:128], ones1[0:1, :], rowq[0:1, 0:128],
                                                   start=True, stop=True), after=[t_um, t_ones1])
        return tr, t_bc

    prev_q = [None]
    prev_cw = [None, None]
    prev_ps = [None]

    def quant_gateup(which, ic, wT_r, cache, qbuf_par):
        qbuf_par = 0
        tl = s.emit("sync", lambda e: e.dma_start(out=wraw[:].rearrange("p (jc i) -> p jc i", jc=HC),
                                                  in_=wT_r[:, :, ic * 128:(ic + 1) * 128]),
                    after=[prev_q[0]] if prev_q[0] is not None else list(init_toks), dma=True, group="ql")
        v_in = wraw[:].rearrange("p (jc i) -> p i jc", jc=HC)
        tm = s.emit("vector", lambda e: e.tensor_reduce(out=Mxq[:, 0:128], in_=v_in, axis=mybir.AxisListType.X,
                                                        op=ALU.max, apply_absolute_value=True),
                    after=[tl, prev_ps[0]])
        tr, t_bc = q_scale_chain(tm, after=[tm])
        sc_t = wsg if which == "g" else wsu
        tsc = s.emit("vector", lambda e: e.tensor_scalar(out=sc_t[:, ic:ic + 1], in0=colsq[:, 0:1],
                                                         scalar1=1.0 / 224.0, scalar2=None, op0=ALU.mult), after=[tr])
        qb = q8g[qbuf_par]
        qts = []
        for jc in range(HC):
            qts.append(s.emit("vector", lambda e, jc=jc: e.tensor_tensor(
                out=qb[:, jc * 128:(jc + 1) * 128], in0=wraw[:, jc * 128:(jc + 1) * 128],
                in1=psq[:, 0:128], op=ALU.mult),
                after=[t_bc] + ([prev_cw[qbuf_par]] if jc == 0 else [qts[-1]])))
        t_cw = s.emit("gpsimd", lambda e, ic=ic: e.dma_start(out=cache[ic], in_=qb[:]), after=[qts[-1]], dma=True, group="qcg")
        prev_q[0] = qts[-1]
        prev_ps[0] = qts[-1]
        prev_cw[qbuf_par] = t_cw
        return t_cw

    prev_dq = [None]
    prev_dcw = [None]

    def quant_down2(hcc):
        t_prev = None
        for q in range(4):
            t_l = s.emit("sync", lambda e, q=q: e.dma_start(
                out=wdraw[:].rearrange("p (icb hh) -> p icb hh", icb=QH),
                in_=wdT_r[:, q * QH:(q + 1) * QH, hcc * 128:(hcc + 1) * 128]),
                after=[t_prev if t_prev is not None else (prev_dq[0] if prev_dq[0] is not None else prev_q[0])],
                dma=True, group="qd")
            v_in = wdraw[:].rearrange("p (icb hh) -> p hh icb", icb=QH)
            if q == 0:
                t_m = s.emit("vector", lambda e: e.tensor_reduce(out=Mxq[:, 0:128], in_=v_in,
                                                                 axis=mybir.AxisListType.X, op=ALU.max,
                                                                 apply_absolute_value=True),
                             after=[t_l, prev_ps[0]])
            else:
                t_r2 = s.emit("vector", lambda e: e.tensor_reduce(out=psq_scr[:, 0:128], in_=v_in,
                                                                  axis=mybir.AxisListType.X, op=ALU.max,
                                                                  apply_absolute_value=True), after=[t_l])
                t_m = s.emit("vector", lambda e: e.tensor_tensor(out=Mxq[:, 0:128], in0=Mxq[:, 0:128],
                                                                 in1=psq_scr[:, 0:128], op=ALU.max), after=[t_r2, t_m])
            t_prev = t_m
        tr, t_bc = q_scale_chain(t_prev, after=[t_prev])
        s.emit("vector", lambda e: e.tensor_scalar(out=wsd[:, hcc:hcc + 1], in0=colsq[:, 0:1],
                                                   scalar1=0.5 / 224.0, scalar2=None, op0=ALU.mult), after=[tr])
        cws = []
        t_data = t_prev
        for q in range(4):
            t_data = s.emit("sync", lambda e, q=q: e.dma_start(
                out=wdraw[:].rearrange("p (icb hh) -> p icb hh", icb=QH),
                in_=wdT_r[:, q * QH:(q + 1) * QH, hcc * 128:(hcc + 1) * 128]),
                after=[t_data, t_bc], dma=True, group="qd")
            qts = []
            for icb in range(QH):
                qts.append(s.emit("vector", lambda e, icb=icb: e.tensor_tensor(
                    out=q8d[:, icb * 128:(icb + 1) * 128], in0=wdraw[:, icb * 128:(icb + 1) * 128],
                    in1=psq[:, 0:128], op=ALU.mult),
                    after=[t_data, t_bc] + ([prev_dcw[0]] if icb == 0 else [qts[-1]])))
            t_cw = s.emit("gpsimd", lambda e, hcc=hcc, q=q: e.dma_start(
                out=wdq[hcc][:, q * QH * 128:(q + 1) * QH * 128], in_=q8d[:]), after=[qts[-1]], dma=True, group="qcd")
            prev_dcw[0] = t_cw
            cws.append(t_cw)
            t_data = qts[-1]
            prev_dq[0] = qts[-1]
            prev_ps[0] = qts[-1]
        return cws

    psq_scr = qsb("q_scr", [128, 128], F32)

    use_ffn = ("ffn1" in phases) or ("ffn2" in phases)
    if use_ffn:
        for ic in range(IC):
            cache_g[ic] = quant_gateup("g", ic, wgT_r, wgq, 0)
            cache_u[ic] = quant_gateup("u", ic, wuT_r, wuq, 1)
        for hcc in range(HC):
            cache_d[hcc] = quant_down2(hcc)

    def ffn(src_r, dst_r, prev_tail, tag):
        ph = contextlib.ExitStack()
        psb = lambda name, shape: ph.enter_context(nc.psum_tensor(name, shape, F32))
        ps_sum = psb(f"{tag}_ps_sum", [128, TT])
        ps_sq = psb(f"{tag}_ps_sq", [128, TT])
        ps_bc = psb(f"{tag}_ps_bc", [128, TT])
        ps_g = [psb(f"{tag}_ps_g0", [128, TT]), psb(f"{tag}_ps_g1", [128, TT])]
        ps_u = [psb(f"{tag}_ps_u0", [128, TT]), psb(f"{tag}_ps_u1", [128, TT])]
        ps_dn = [ps_sq, ps_sum]

        xin = ph.enter_context(nc.sbuf_tensor(f"{tag}_xin", [128, HC * TT], F32))
        xbb = [ph.enter_context(nc.sbuf_tensor(f"{tag}_xbb{k}", [128, TT], BF16)) for k in range(2)]
        qx = ph.enter_context(nc.sbuf_tensor(f"{tag}_qx", [128, HC * TT], FP8))
        t0 = ph.enter_context(nc.sbuf_tensor(f"{tag}_t0", [128, IC * TT], F32))
        qt = ph.enter_context(nc.sbuf_tensor(f"{tag}_qt", [128, IC * TT], FP8))
        scr1 = ph.enter_context(nc.sbuf_tensor(f"{tag}_scr1", [128, TT], F32))
        scr2 = ph.enter_context(nc.sbuf_tensor(f"{tag}_scr2", [128, TT], F32))
        Mx = ph.enter_context(nc.sbuf_tensor(f"{tag}_Mx", [128, TT], F32))
        cols = ph.enter_context(nc.sbuf_tensor(f"{tag}_cols", [128, 8], F32))
        rows = ph.enter_context(nc.sbuf_tensor(f"{tag}_rows", [1, 2 * TT], F32))
        bsc = ph.enter_context(nc.sbuf_tensor(f"{tag}_bsc", [128, TT], F32))
        qwg = [ph.enter_context(nc.sbuf_tensor(f"{tag}_qwg{k}", [128, HC * 128], FP8)) for k in range(2)]
        qwu = [ph.enter_context(nc.sbuf_tensor(f"{tag}_qwu{k}", [128, HC * 128], FP8)) for k in range(2)]
        qwd = [ph.enter_context(nc.sbuf_tensor(f"{tag}_qwd{k}", [128, IC * 128], FP8)) for k in range(2)]

        tail = list(prev_tail)

        def maxcol_bcast(after, extra_row_fn, consume_fn):
            nb = TT // 128
            toks = []
            for b in range(nb):
                tt = s.emit("tensor", lambda e, b=b: e.transpose(ps_bc[:, 0:128], Mx[:, b * 128:(b + 1) * 128], idt[:]),
                            after=after + [t_ident] + ([toks[-1]] if toks else []))
                rd = s.emit("vector", lambda e, b=b: e.tensor_reduce(out=cols[:, b:b + 1], in_=ps_bc[:, 0:128],
                                                                     axis=mybir.AxisListType.X, op=ALU.max),
                            after=[tt])
                toks.append(rd)
            row_toks = []
            for b in range(nb):
                t1 = s.emit("tensor", lambda e, b=b: e.transpose(ps_bc[0:1, 0:128], cols[:, b:b + 1], idt[:]),
                            after=[toks[-1]] + ([row_toks[-1]] if row_toks else []))
                c1 = s.emit("vector", lambda e, b=b: e.tensor_copy(out=rows[0:1, b * 128:(b + 1) * 128], in_=ps_bc[0:1, 0:128]),
                            after=[t1])
                row_toks.append(c1)
            t_ln = s.emit("scalar", lambda e: e.activation(out=rows[0:1, TT:2 * TT], in_=rows[0:1, 0:TT],
                                                           func=AF.Ln), after=row_toks)
            t_ex = s.emit("scalar", lambda e: e.activation(out=rows[0:1, TT:2 * TT], in_=rows[0:1, TT:2 * TT],
                                                           func=AF.Exp, scale=-1.0), after=[t_ln])
            t_uc = s.emit("vector", lambda e: e.tensor_scalar(out=rows[0:1, TT:2 * TT], in0=rows[0:1, TT:2 * TT],
                                                              scalar1=224.0, scalar2=None, op0=ALU.mult), after=[t_ex])
            tb = s.emit("tensor", lambda e: e.matmul(ps_bc[:, 0:TT], ones1[0:1, :], rows[0:1, TT:2 * TT], start=True, stop=True),
                        after=[t_uc, t_ones1])
            t_cons = consume_fn(tb)
            r_last = extra_row_fn([row_toks[-1], tb])
            tb2 = s.emit("tensor", lambda e: e.matmul(ps_bc[:, 0:TT], ones1[0:1, :], rows[0:1, TT:2 * TT], start=True, stop=True),
                         after=[r_last, t_cons])
            t_bsc = s.emit("vector", lambda e: e.tensor_copy(out=bsc[:, 0:TT], in_=ps_bc[:, 0:TT]), after=[tb2])
            return t_cons, t_bsc

        guard0 = prev_tail[-1] if prev_tail else None
        wg_rd = [guard0, guard0]
        wu_rd = [guard0, guard0]
        wd_rd = [guard0, guard0]
        eplg1 = [None, None]
        eplg3 = [None, None]
        d_ep1 = [None, None]
        st_prev = [None, None]

        for tile_i in range(NTILE):
            tk0 = tile_i * TT
            t_x = s.emit("sync", lambda e, tk0=tk0: e.dma_start(
                out=xin[:].rearrange("p (hc t) -> p hc t", hc=HC),
                in_=src_r[:, :, tk0:tk0 + TT]), after=tail, dma=True, group=tag + "x")
            mm_s = []
            mm_q = []
            for hc in range(HC):
                par = hc % 2
                tc = s.emit("scalar", lambda e, hc=hc, par=par: e.activation(
                    out=xbb[par][:], in_=xin[:, hc * TT:(hc + 1) * TT], func=AF.Copy),
                    after=[t_x] + ([mm_q[-2]] if len(mm_q) >= 2 else []))
                ms = s.emit("tensor", lambda e, hc=hc, par=par: e.matmul(
                    ps_sum[:, 0:TT], ones128[:], xbb[par][:], start=(hc == 0), stop=(hc == HC - 1)),
                    after=[tc, t_ones128])
                tq = s.emit("scalar", lambda e, hc=hc, par=par: e.activation(
                    out=xbb[par][:], in_=xin[:, hc * TT:(hc + 1) * TT], func=AF.Square),
                    after=[ms])
                mq = s.emit("tensor", lambda e, hc=hc, par=par: e.matmul(
                    ps_sq[:, 0:TT], ones128[:], xbb[par][:], start=(hc == 0), stop=(hc == HC - 1)),
                    after=[tq])
                mm_s.append(ms)
                mm_q.append(mq)
            t_sum = mm_s[-1]
            t_sqm = mm_q[-1]
            tree = []
            for hc in range(HC):
                td = s.emit("vector", lambda e, hc=hc: e.scalar_tensor_tensor(
                    out=scr1[:, 0:TT], in0=ps_sum[:, 0:TT], scalar=-1.0 / 2048.0,
                    in1=xin[:, hc * TT:(hc + 1) * TT], op0=ALU.mult, op1=ALU.add),
                    after=[t_sum] + ([tree[-1]] if tree else []))
                ta = s.emit("scalar", lambda e: e.activation(out=scr2[:, 0:TT], in_=scr1[:, 0:TT], func=AF.Abs),
                            after=[td])
                if hc == 0:
                    tm = s.emit("vector", lambda e: e.tensor_copy(out=Mx[:, 0:TT], in_=scr2[:, 0:TT]), after=[ta])
                else:
                    tm = s.emit("vector", lambda e: e.tensor_tensor(out=Mx[:, 0:TT], in0=Mx[:, 0:TT],
                                                                    in1=scr2[:, 0:TT], op=ALU.max), after=[ta])
                tree.append(tm)

            def xrow_fn(aft):
                r1 = s.emit("vector", lambda e: e.tensor_scalar(out=rows[0:1, TT:2 * TT], in0=ps_sum[0:1, 0:TT],
                                                                scalar1=1.0 / 2048.0, scalar2=None, op0=ALU.mult),
                            after=[t_sum] + aft)
                r2 = s.emit("vector", lambda e: e.tensor_tensor(out=rows[0:1, TT:2 * TT], in0=rows[0:1, TT:2 * TT],
                                                                in1=rows[0:1, TT:2 * TT], op=ALU.mult), after=[r1])
                r3 = s.emit("vector", lambda e: e.scalar_tensor_tensor(out=rows[0:1, TT:2 * TT], in0=ps_sq[0:1, 0:TT],
                                                                       scalar=1.0 / 2048.0, in1=rows[0:1, TT:2 * TT],
                                                                       op0=ALU.mult, op1=ALU.subtract),
                            after=[r2, t_sqm])
                r4 = s.emit("vector", lambda e: e.tensor_scalar(out=rows[0:1, TT:2 * TT], in0=rows[0:1, TT:2 * TT],
                                                                scalar1=1e-5, scalar2=None, op0=ALU.add), after=[r3])
                r5 = s.emit("scalar", lambda e: e.activation(out=rows[0:1, TT:2 * TT], in_=rows[0:1, TT:2 * TT], func=AF.Ln), after=[r4])
                r6 = s.emit("scalar", lambda e: e.activation(out=rows[0:1, TT:2 * TT], in_=rows[0:1, TT:2 * TT], func=AF.Exp,
                                                             scale=-0.5), after=[r5])
                r7 = s.emit("vector", lambda e: e.tensor_tensor(out=rows[0:1, TT:2 * TT], in0=rows[0:1, 0:TT],
                                                                in1=rows[0:1, TT:2 * TT], op=ALU.mult), after=[r6])
                r8 = s.emit("vector", lambda e: e.tensor_scalar(out=rows[0:1, TT:2 * TT], in0=rows[0:1, TT:2 * TT],
                                                                scalar1=1.0 / 224.0, scalar2=None, op0=ALU.mult), after=[r7])
                return r8

            def qx_consume(tb):
                toks = []
                for hc in range(HC):
                    td = s.emit("vector", lambda e, hc=hc: e.scalar_tensor_tensor(
                        out=scr1[:, 0:TT], in0=ps_sum[:, 0:TT], scalar=-1.0 / 2048.0,
                        in1=xin[:, hc * TT:(hc + 1) * TT], op0=ALU.mult, op1=ALU.add),
                        after=[tb] + ([toks[-1]] if toks else []))
                    tqx = s.emit("vector", lambda e, hc=hc: e.tensor_tensor(
                        out=qx[:, hc * TT:(hc + 1) * TT], in0=scr1[:, 0:TT], in1=ps_bc[:, 0:TT], op=ALU.mult),
                        after=[td])
                    toks.append(tqx)
                return toks[-1]

            t_qx, tb_bxs = maxcol_bcast(after=[tree[-1]], extra_row_fn=xrow_fn, consume_fn=qx_consume)

            tMt = None
            for ic in range(IC):
                par = ic % 2
                wg_tok = s.emit("sync", lambda e, ic=ic, par=par: e.dma_start(out=qwg[par][:], in_=wgq[ic]),
                                after=[cache_g[ic], wg_rd[par]], dma=True, group=f"{tag}g{par}")
                wu_tok = s.emit("sync", lambda e, ic=ic, par=par: e.dma_start(out=qwu[par][:], in_=wuq[ic]),
                                after=[cache_u[ic], wu_rd[par]], dma=True, group=f"{tag}u{par}")
                mmg = []
                for jc2 in range(HC // 2):
                    mmg.append(s.emit("tensor", lambda e, jc2=jc2, par=par: e.matmul(
                        ps_g[par][:, 0:TT],
                        qwg[par][:, jc2 * 256:(jc2 + 1) * 256].rearrange("p (i m) -> p i m", i=2),
                        qx[:, jc2 * 2 * TT:(jc2 + 1) * 2 * TT].rearrange("p (i n) -> p i n", i=2),
                        start=(jc2 == 0), stop=(jc2 == HC // 2 - 1),
                        perf_mode=mybir.MatmulPerfMode.DoubleRow),
                        after=[t_qx, wg_tok, eplg1[par]] + ([mmg[-1]] if mmg else [])))
                mmu = []
                for jc2 in range(HC // 2):
                    mmu.append(s.emit("tensor", lambda e, jc2=jc2, par=par: e.matmul(
                        ps_u[par][:, 0:TT],
                        qwu[par][:, jc2 * 256:(jc2 + 1) * 256].rearrange("p (i m) -> p i m", i=2),
                        qx[:, jc2 * 2 * TT:(jc2 + 1) * 2 * TT].rearrange("p (i n) -> p i n", i=2),
                        start=(jc2 == 0), stop=(jc2 == HC // 2 - 1),
                        perf_mode=mybir.MatmulPerfMode.DoubleRow),
                        after=[wu_tok, eplg3[par], mmg[-1]] + ([mmu[-1]] if mmu else [])))
                wg_rd[par] = mmg[-1]
                wu_rd[par] = mmu[-1]
                e1 = s.emit("vector", lambda e, ic=ic, par=par: e.scalar_tensor_tensor(
                    out=scr1[:, 0:TT], in0=ps_g[par][:, 0:TT], scalar=wsg[:, ic:ic + 1], in1=bsc[:, 0:TT],
                    op0=ALU.mult, op1=ALU.mult), after=[mmg[-1], tb_bxs])
                e2 = s.emit("scalar", lambda e: e.activation(out=scr1[:, 0:TT], in_=scr1[:, 0:TT], func=AF.Silu),
                            after=[e1])
                e3 = s.emit("vector", lambda e, ic=ic, par=par: e.scalar_tensor_tensor(
                    out=scr2[:, 0:TT], in0=ps_u[par][:, 0:TT], scalar=wsu[:, ic:ic + 1], in1=bsc[:, 0:TT],
                    op0=ALU.mult, op1=ALU.mult), after=[mmu[-1], e1])
                e4 = s.emit("vector", lambda e, ic=ic: e.tensor_tensor(
                    out=t0[:, ic * TT:(ic + 1) * TT], in0=scr1[:, 0:TT], in1=scr2[:, 0:TT], op=ALU.mult),
                    after=[e2, e3])
                eplg1[par] = e1
                eplg3[par] = e3
                ta = s.emit("scalar", lambda e, ic=ic: e.activation(out=scr2[:, 0:TT],
                                                                    in_=t0[:, ic * TT:(ic + 1) * TT], func=AF.Abs),
                            after=[e4])
                if ic == 0:
                    tMt = s.emit("vector", lambda e: e.tensor_copy(out=Mx[:, 0:TT], in_=scr2[:, 0:TT]),
                                 after=[ta, tb_bxs])
                else:
                    tMt = s.emit("vector", lambda e: e.tensor_tensor(out=Mx[:, 0:TT], in0=Mx[:, 0:TT],
                                                                     in1=scr2[:, 0:TT], op=ALU.max), after=[ta, tMt])

            def trow_fn(aft):
                r8 = s.emit("vector", lambda e: e.tensor_scalar(out=rows[0:1, TT:2 * TT], in0=rows[0:1, 0:TT],
                                                                scalar1=1.0 / 224.0, scalar2=None, op0=ALU.mult),
                            after=aft)
                return r8

            def qt_consume(tb):
                toks = []
                for ic in range(IC):
                    toks.append(s.emit("vector", lambda e, ic=ic: e.tensor_tensor(
                        out=qt[:, ic * TT:(ic + 1) * TT], in0=t0[:, ic * TT:(ic + 1) * TT],
                        in1=ps_bc[:, 0:TT], op=ALU.mult),
                        after=[tb] + ([toks[-1]] if toks else [])))
                return toks[-1]

            t_qt, tb_bts = maxcol_bcast(after=[tMt], extra_row_fn=trow_fn, consume_fn=qt_consume)

            st_toks = []
            for hcc in range(HC):
                par = hcc % 2
                wd_tok = s.emit("sync", lambda e, hcc=hcc, par=par: e.dma_start(out=qwd[par][:], in_=wdq[hcc]),
                                after=(cache_d[hcc] if isinstance(cache_d[hcc], list) else [cache_d[hcc]]) + [wd_rd[par]],
                                dma=True, group=f"{tag}d{par}")
                mmd = []
                for icb2 in range(IC // 2):
                    mmd.append(s.emit("tensor", lambda e, icb2=icb2, par=par: e.matmul(
                        ps_dn[par][:, 0:TT],
                        qwd[par][:, icb2 * 256:(icb2 + 1) * 256].rearrange("p (i m) -> p i m", i=2),
                        qt[:, icb2 * 2 * TT:(icb2 + 1) * 2 * TT].rearrange("p (i n) -> p i n", i=2),
                        start=(icb2 == 0), stop=(icb2 == IC // 2 - 1),
                        perf_mode=mybir.MatmulPerfMode.DoubleRow),
                        after=[t_qt, wd_tok, d_ep1[par]] + ([mmd[-1]] if mmd else [])))
                wd_rd[par] = mmd[-1]
                ep1 = s.emit("vector", lambda e, hcc=hcc, par=par: e.scalar_tensor_tensor(
                    out=scr1[:, 0:TT], in0=ps_dn[par][:, 0:TT], scalar=wsd[:, hcc:hcc + 1], in1=bsc[:, 0:TT],
                    op0=ALU.mult, op1=ALU.mult),
                    after=[mmd[-1], tb_bts])
                ep2 = s.emit("vector", lambda e, hcc=hcc: e.tensor_tensor(
                    out=xin[:, hcc * TT:(hcc + 1) * TT], in0=scr1[:, 0:TT],
                    in1=xin[:, hcc * TT:(hcc + 1) * TT], op=ALU.add),
                    after=[ep1])
                d_ep1[par] = ep1
                st = s.emit("scalar", lambda e, hcc=hcc, tk0=tk0: e.dma_start(
                    out=dst_r[:, hcc, tk0:tk0 + TT], in_=xin[:, hcc * TT:(hcc + 1) * TT]),
                    after=[ep2], dma=True, force=True, group=tag + "s")
                st_toks.append(st)
            tail = [st_toks[-1]]

        if DBG_FFN1 and tag == "f1":
            def dbg_out(name, ap, dt=F32, cast=False):
                od = nc.dram_tensor("dbg_" + name, list(ap.shape), dt, kind="ExternalOutput").ap()
                eng = "gpsimd" if cast else "sync"
                s.emit(eng, lambda e, od=od, ap=ap: e.dma_start(out=od, in_=ap),
                       after=list(tail), dma=True, force=True, group="dbg")
            dbg_out("wsg", wsg[:])
            dbg_out("wsu", wsu[:])
            dbg_out("wsd", wsd[:])
            dbg_out("rows", rows[0:1, :])
            dbg_out("bsc", bsc[:])
            dbg_out("Mx", Mx[:])
            dbg_out("qx", qx[:], cast=True)
            dbg_out("qwg0", qwg[0][:], cast=True)
            dbg_out("qwd0", qwd[0][:], cast=True)
            dbg_out("wgq0", wgq[0], cast=True)
            dbg_out("wgq40", wgq[40], cast=True)
            dbg_out("wdq0", wdq[0], cast=True)
            dbg_out("t0a", t0[:, 0:4 * TT])
            dbg_out("qta", qt[:, 0:4 * TT], cast=True)

        ph.close()
        return tail

    def attention(prev_tail):
        ph = contextlib.ExitStack()
        psb = lambda name, shape: ph.enter_context(nc.psum_tensor(name, shape, F32))
        ps_big = [psb("a_big0", [128, 2048]), psb("a_big1", [128, 2048])]
        xn = ph.enter_context(nc.sbuf_tensor("a_xn", [128, HC * XSTR], BF16))
        kv = ph.enter_context(nc.sbuf_tensor("a_kv", [128, HC * 2048], BF16))
        pool = [ph.enter_context(nc.sbuf_tensor(f"a_pl{k}", [128, 2048], F32)) for k in range(5)]
        xa = [ph.enter_context(nc.sbuf_tensor(f"a_xa{k}", [128, 512], F32)) for k in range(2)]
        xbb0 = ph.enter_context(nc.sbuf_tensor("a_xbb0", [128, 512], BF16))
        xbb = [xbb0, xbb0]
        scr = ph.enter_context(nc.sbuf_tensor("a_scr", [128, 512], F32))
        rsc = ph.enter_context(nc.sbuf_tensor("a_rsc", [1, 1024], F32))
        binv = ph.enter_context(nc.sbuf_tensor("a_binv", [128, 512], F32))
        yst = [scr, binv]
        wa = [ph.enter_context(nc.sbuf_tensor(f"a_wa{k}", [128, 2 * HC * 128], BF16)) for k in range(2)]
        wv = [ph.enter_context(nc.sbuf_tensor(f"a_wv{k}", [128, HC * 128], BF16)) for k in range(2)]

        tail = list(prev_tail)

        t_z = None
        for hc in range(HC):
            t_z = s.emit("vector", lambda e, hc=hc: e.memset(xn[:, hc * XSTR:hc * XSTR + 1], 0.0), after=tail)

        ln_prev = None
        xa_rd = [None, None]
        for tb in range(4):
            tk0 = tb * 512
            mm_s = []
            mm_q = []
            for hc in range(HC):
                par = hc % 2
                t_l = s.emit("sync", lambda e, hc=hc, par=par, tk0=tk0: e.dma_start(
                    out=xa[par][:], in_=resid_r[:, hc, tk0:tk0 + 512]),
                    after=([xa_rd[par]] if xa_rd[par] is not None else tail) + ([ln_prev] if (hc < 2 and ln_prev) else []),
                    dma=True, group=f"ax{par}")
                tc = s.emit("scalar", lambda e, par=par: e.activation(out=xbb[par][:], in_=xa[par][:], func=AF.Copy),
                            after=[t_l] + ([mm_q[-1]] if mm_q else []))
                ms = s.emit("tensor", lambda e, hc=hc, par=par: e.matmul(
                    ps_big[0][:, 0:512], ones128[:], xbb[par][:], start=(hc == 0), stop=(hc == HC - 1)),
                    after=[tc, t_ones128] + ([ln_prev] if (hc == 0 and ln_prev) else []))
                tq = s.emit("scalar", lambda e, par=par: e.activation(out=xbb[par][:], in_=xa[par][:], func=AF.Square),
                            after=[ms])
                mq = s.emit("tensor", lambda e, hc=hc, par=par: e.matmul(
                    ps_big[0][:, 512:1024], ones128[:], xbb[par][:], start=(hc == 0), stop=(hc == HC - 1)),
                    after=[tq])
                mm_s.append(ms)
                mm_q.append(mq)
                xa_rd[par] = tq
            r1 = s.emit("vector", lambda e: e.tensor_scalar(out=rsc[0:1, 0:512], in0=ps_big[0][0:1, 0:512],
                                                            scalar1=1.0 / 2048.0, scalar2=None, op0=ALU.mult),
                        after=[mm_s[-1]])
            r2 = s.emit("vector", lambda e: e.tensor_tensor(out=rsc[0:1, 512:1024], in0=rsc[0:1, 0:512],
                                                            in1=rsc[0:1, 0:512], op=ALU.mult), after=[r1])
            r3 = s.emit("vector", lambda e: e.scalar_tensor_tensor(out=rsc[0:1, 512:1024], in0=ps_big[0][0:1, 512:1024],
                                                                   scalar=1.0 / 2048.0, in1=rsc[0:1, 512:1024],
                                                                   op0=ALU.mult, op1=ALU.subtract), after=[r2, mm_q[-1]])
            r4 = s.emit("vector", lambda e: e.tensor_scalar(out=rsc[0:1, 512:1024], in0=rsc[0:1, 512:1024], scalar1=1e-5,
                                                            scalar2=None, op0=ALU.add), after=[r3])
            r5 = s.emit("scalar", lambda e: e.activation(out=rsc[0:1, 512:1024], in_=rsc[0:1, 512:1024], func=AF.Ln), after=[r4])
            r6 = s.emit("scalar", lambda e: e.activation(out=rsc[0:1, 512:1024], in_=rsc[0:1, 512:1024], func=AF.Exp,
                                                         scale=-0.5), after=[r5])
            tb1 = s.emit("tensor", lambda e: e.matmul(ps_big[1][:, 0:512], ones1[0:1, :], rsc[0:1, 512:1024],
                                                      start=True, stop=True),
                         after=[r6, t_ones1] + ([ln_prev] if ln_prev else []))
            tb2 = s.emit("vector", lambda e: e.tensor_copy(out=binv[:, 0:512], in_=ps_big[1][:, 0:512]), after=[tb1])
            for hc in range(HC):
                par = hc % 2
                t_l2 = s.emit("sync", lambda e, hc=hc, par=par, tk0=tk0: e.dma_start(
                    out=xa[par][:], in_=resid_r[:, hc, tk0:tk0 + 512]),
                    after=[xa_rd[par]], dma=True, group=f"ax{par}")
                std = s.emit("vector", lambda e, par=par: e.scalar_tensor_tensor(
                    out=scr[:, 0:512], in0=ps_big[0][:, 0:512], scalar=-1.0 / 2048.0,
                    in1=xa[par][:], op0=ALU.mult, op1=ALU.add), after=[t_l2, tb2])
                tw = s.emit("vector", lambda e, hc=hc, tb=tb: e.tensor_tensor(
                    out=xn[:, hc * XSTR + 1 + tb * 512: hc * XSTR + 1 + tb * 512 + 512],
                    in0=scr[:, 0:512], in1=binv[:, 0:512], op=ALU.mult), after=[std, t_z])
                xa_rd[par] = std
                ln_prev = tw
        t_ln = ln_prev

        kv_toks = {}
        pair_planes = {}
        wa_ld = [None, None]
        wa_rd = [None, None]
        evac_done = [None, None]
        scan_chain = []
        oc_order = []
        for j in range(8):
            oc_order += [j, j + 8]
        for idx, oc in enumerate(oc_order):
            par = idx % 2
            t_w = s.emit("gpsimd", lambda e, oc=oc, par=par: e.dma_start(
                out=wa[par][:].rearrange("p (kc c) -> p kc c", kc=2 * HC),
                in_=waT_r[:, :, oc * 128:(oc + 1) * 128]),
                after=[wa_ld[par], wa_rd[par]] + ([] if wa_ld[par] else tail), dma=True, group=f"awa{par}")
            wa_ld[par] = t_w
            mm = []
            for tb in range(4):
                for kc in range(2 * HC):
                    hcs = kc if kc < HC else kc - HC
                    off = 0 if kc < HC else 1
                    mm.append(s.emit("tensor", lambda e, kc=kc, tb=tb, par=par, hcs=hcs, off=off: e.matmul(
                        ps_big[par][:, tb * 512:(tb + 1) * 512],
                        wa[par][:, kc * 128:(kc + 1) * 128],
                        xn[:, hcs * XSTR + off + tb * 512: hcs * XSTR + off + tb * 512 + 512],
                        start=(kc == 0), stop=(kc == 2 * HC - 1)),
                        after=[t_w, t_ln, evac_done[par]] + ([mm[-1]] if mm else [])))
            wa_rd[par] = mm[-1]
            plane = pool[0] if oc < 8 else pool[1]
            ev1 = s.emit("scalar", lambda e, par=par, plane=plane: e.activation(
                out=plane[:], in_=ps_big[par][:], func=AF.Sigmoid),
                after=[mm[-1]] + scan_chain[-1:])
            ev2 = s.emit("vector", lambda e, plane=plane: e.tensor_scalar(
                out=plane[:], in0=plane[:], scalar1=0.5, scalar2=None, op0=ALU.max), after=[ev1])
            evac_done[par] = ev2
            pair_planes[oc] = ev2
            if oc >= 8:
                jj = oc - 8
                A, B, Cp, D, Fm = pool[0], pool[1], pool[2], pool[3], pool[4]
                dep = [pair_planes[jj], pair_planes[oc]] + scan_chain[-1:]
                t1 = s.emit("vector", lambda e: e.tensor_tensor(out=Fm[:], in0=B[:], in1=A[:], op=ALU.is_gt), after=dep)
                t2 = s.emit("scalar", lambda e: e.activation(out=Cp[:], in_=A[:], func=AF.Square), after=dep)
                t3 = s.emit("scalar", lambda e: e.activation(out=D[:], in_=B[:], func=AF.Square), after=[t2])
                t4 = s.emit("vector", lambda e: e.tensor_tensor(out=Cp[:], in0=Cp[:], in1=D[:], op=ALU.add), after=[t1, t3])
                t5 = s.emit("scalar", lambda e: e.activation(out=Cp[:], in_=Cp[:], func=AF.Ln), after=[t4])
                t6 = s.emit("vector", lambda e: e.tensor_tensor_scan(out=D[:], data0=Cp[:], data1=Cp[:],
                                                                     initial=-1e30, op0=ALU.max, op1=ALU.bypass), after=[t5])
                t7 = s.emit("vector", lambda e: e.tensor_tensor(out=D[:], in0=Cp[:], in1=D[:], op=ALU.subtract), after=[t6])
                t8 = s.emit("vector", lambda e: e.tensor_tensor_scan(out=D[:], data0=D[:], data1=D[:],
                                                                     initial=0.0, op0=ALU.add, op1=ALU.bypass), after=[t7])
                t9 = s.emit("scalar", lambda e: e.activation(out=D[:], in_=D[:], func=AF.Exp, scale=0.5), after=[t8])
                t10 = s.emit("vector", lambda e: e.tensor_tensor(out=Cp[:], in0=A[:], in1=B[:], op=ALU.max), after=[t7])
                t11 = s.emit("scalar", lambda e: e.activation(out=Cp[:], in_=Cp[:], func=AF.Ln), after=[t10, t9])
                t12 = s.emit("scalar", lambda e: e.activation(out=Cp[:], in_=Cp[:], func=AF.Exp, scale=-1.0), after=[t11])
                t13 = s.emit("vector", lambda e: e.tensor_tensor(out=B[:], in0=A[:], in1=B[:], op=ALU.min), after=[t10])
                t14 = s.emit("vector", lambda e: e.tensor_tensor(out=Cp[:], in0=B[:], in1=Cp[:], op=ALU.mult), after=[t12, t13])
                t15 = s.emit("scalar", lambda e: e.activation(out=Cp[:], in_=Cp[:], func=AF.Arctan), after=[t14])
                u1 = s.emit("vector", lambda e: e.tensor_tensor(out=A[:], in0=Fm[:], in1=Cp[:], op=ALU.mult), after=[t15])
                u2 = s.emit("vector", lambda e: e.scalar_tensor_tensor(out=A[:], in0=A[:], scalar=-2.0,
                                                                       in1=Cp[:], op0=ALU.mult, op1=ALU.add), after=[u1])
                u3 = s.emit("vector", lambda e: e.scalar_tensor_tensor(out=A[:], in0=Fm[:], scalar=float(np.pi / 2),
                                                                       in1=A[:], op0=ALU.mult, op1=ALU.add), after=[u2])
                t19 = s.emit("vector", lambda e: e.tensor_tensor_scan(out=A[:], data0=A[:], data1=A[:],
                                                                      initial=0.0, op0=ALU.add, op1=ALU.bypass), after=[u3])
                c1 = s.emit("vector", lambda e: e.tensor_scalar(out=Cp[:], in0=A[:], scalar1=INV2PI, scalar2=0.25,
                                                                op0=ALU.mult, op1=ALU.add), after=[t19])
                c2 = s.emit("vector", lambda e: e.tensor_scalar(out=B[:], in0=Cp[:], scalar1=MAGIC, scalar2=None,
                                                                op0=ALU.add), after=[c1])
                c3 = s.emit("vector", lambda e: e.tensor_scalar(out=B[:], in0=B[:], scalar1=MAGIC, scalar2=None,
                                                                op0=ALU.subtract), after=[c2])
                c4 = s.emit("vector", lambda e: e.tensor_tensor(out=Cp[:], in0=Cp[:], in1=B[:], op=ALU.subtract), after=[c3])
                c5 = s.emit("scalar", lambda e: e.activation(out=Cp[:], in_=Cp[:], func=AF.Sin, scale=TWO_PI), after=[c4])
                s1 = s.emit("vector", lambda e: e.tensor_scalar(out=B[:], in0=A[:], scalar1=INV2PI, scalar2=MAGIC,
                                                                op0=ALU.mult, op1=ALU.add), after=[c4])
                s2 = s.emit("vector", lambda e: e.tensor_scalar(out=B[:], in0=B[:], scalar1=MAGIC, scalar2=None,
                                                                op0=ALU.subtract), after=[s1])
                s3 = s.emit("vector", lambda e: e.scalar_tensor_tensor(out=B[:], in0=A[:], scalar=INV2PI,
                                                                       in1=B[:], op0=ALU.mult, op1=ALU.subtract), after=[s2])
                s4 = s.emit("scalar", lambda e: e.activation(out=B[:], in_=B[:], func=AF.Sin, scale=TWO_PI), after=[s3, c5])
                k1 = s.emit("vector", lambda e, jj=jj: e.tensor_tensor(out=kv[:, jj * 2048:(jj + 1) * 2048],
                                                                       in0=D[:], in1=Cp[:], op=ALU.mult), after=[s4, t9])
                k2 = s.emit("vector", lambda e, jj=jj: e.tensor_tensor(out=kv[:, (jj + 8) * 2048:(jj + 9) * 2048],
                                                                       in0=D[:], in1=B[:], op=ALU.mult), after=[k1])
                kv_toks[jj] = k2
                scan_chain.append(k2)

        t_kv = scan_chain[-1]

        wv_ld = [None, None]
        wr_ld = [None, None]
        val_rd = [t_kv, t_kv]
        rec_rd = [t_kv, t_kv]
        ev_done = [None, None]
        st_toks = []
        xa2_rd = [None, None]
        for oc in range(HC):
            par = oc % 2
            t_wv = s.emit("gpsimd", lambda e, oc=oc, par=par: e.dma_start(
                out=wv[par][:].rearrange("p (kc c) -> p kc c", kc=HC),
                in_=wvT_r[:, :, oc * 128:(oc + 1) * 128]),
                after=[wv_ld[par] or t_kv, val_rd[par]], dma=True, group=f"awv{par}")
            wv_ld[par] = t_wv
            mmv = []
            for tb in range(4):
                for kc in range(HC):
                    mmv.append(s.emit("tensor", lambda e, kc=kc, tb=tb, par=par: e.matmul(
                        ps_big[par][:, tb * 512:(tb + 1) * 512],
                        wv[par][:, kc * 128:(kc + 1) * 128],
                        kv[:, kc * 2048 + tb * 512: kc * 2048 + tb * 512 + 512],
                        start=(kc == 0), stop=(kc == HC - 1)),
                        after=[t_wv, t_kv, ev_done[par]] + ([mmv[-1]] if mmv else [])))
            val_rd[par] = mmv[-1]
            t_wr = s.emit("gpsimd", lambda e, oc=oc, par=par: e.dma_start(
                out=wa[par][:].rearrange("p (kc c) -> p kc c", kc=2 * HC),
                in_=wrT_r[:, :, oc * 128:(oc + 1) * 128]),
                after=[wr_ld[par] or t_kv, rec_rd[par] if wr_ld[par] else wa_rd[par]], dma=True, group=f"awa{par}")
            wr_ld[par] = t_wr
            evv = s.emit("vector", lambda e, par=par: e.tensor_copy(out=pool[3][:], in_=ps_big[par][:]),
                         after=[mmv[-1]])
            mmr = []
            for tb in range(4):
                for kc in range(2 * HC):
                    hcs = kc if kc < HC else kc - HC
                    off = 0 if kc < HC else 1
                    mmr.append(s.emit("tensor", lambda e, kc=kc, tb=tb, par=par, hcs=hcs, off=off: e.matmul(
                        ps_big[par][:, tb * 512:(tb + 1) * 512],
                        wa[par][:, kc * 128:(kc + 1) * 128],
                        xn[:, hcs * XSTR + off + tb * 512: hcs * XSTR + off + tb * 512 + 512],
                        start=(kc == 0), stop=(kc == 2 * HC - 1)),
                        after=[t_wr, evv] + ([mmr[-1]] if mmr else [])))
            rec_rd[par] = mmr[-1]
            er1 = s.emit("scalar", lambda e, par=par: e.activation(out=pool[4][:], in_=ps_big[par][:], func=AF.Sigmoid),
                         after=[mmr[-1]])
            er2 = s.emit("vector", lambda e: e.tensor_scalar(out=pool[4][:], in0=pool[4][:], scalar1=0.5,
                                                             scalar2=None, op0=ALU.max), after=[er1])
            ev_done[par] = er2
            er3 = s.emit("vector", lambda e: e.tensor_tensor(out=pool[4][:], in0=pool[4][:], in1=pool[4][:],
                                                             op=ALU.mult), after=[er2])
            er4 = s.emit("vector", lambda e: e.tensor_tensor(out=pool[4][:], in0=pool[4][:], in1=pool[4][:],
                                                             op=ALU.mult), after=[er3])
            er5 = s.emit("vector", lambda e: e.tensor_tensor(out=pool[3][:], in0=pool[3][:], in1=pool[4][:],
                                                             op=ALU.mult), after=[er4])
            for tb in range(4):
                sp = tb % 2
                t_x1 = s.emit("sync", lambda e, oc=oc, tb=tb, sp=sp: e.dma_start(
                    out=xa[sp][:], in_=resid_r[:, oc, tb * 512:(tb + 1) * 512]),
                    after=[er5, xa2_rd[sp] if xa2_rd[sp] is not None else xa_rd[sp]], dma=True, group=f"ax{sp}")
                ta = s.emit("vector", lambda e, tb=tb, sp=sp: e.tensor_tensor(
                    out=yst[sp][:], in0=pool[3][:, tb * 512:(tb + 1) * 512], in1=xa[sp][:], op=ALU.add),
                    after=[t_x1] + ([st_toks[-2]] if len(st_toks) >= 2 else []))
                st = s.emit("scalar", lambda e, oc=oc, tb=tb, sp=sp: e.dma_start(
                    out=resid_r[:, oc, tb * 512:(tb + 1) * 512], in_=yst[sp][:]), after=[ta], dma=True, force=True, group="as")
                xa2_rd[sp] = ta
                st_toks.append(st)
        ph.close()
        return [st_toks[-1]]

    tail = list(init_toks)
    if "ffn1" in phases:
        tail = ffn(xT_r, resid_r if ("attn" in phases or "ffn2" in phases) else outT_r,
                   prev_tail=tail, tag="f1")
    qstack.close()
    if "attn" in phases:
        tail = attention(tail)
    if "ffn2" in phases:
        tail = ffn(resid_r, outT_r, prev_tail=tail, tag="f2")

    s.build(nc, exit_engine="sync", exit_waits=tuple(t for t in tail if t is not None))
    gstack.close()
    return nc


_PERM = np.concatenate([np.arange(0, C, 2), np.arange(1, C, 2)])


def _prep_weights(L0_attn_W_act, L0_attn_W_val, L0_attn_W_rec, L0_ffn_w_gate,
                  L0_ffn_w_up, L0_ffn_w_down, L0_ffn_down_reorder_index):
    wa = np.ascontiguousarray(np.asarray(L0_attn_W_act, dtype=np.float32)[_PERM, :].T)
    wr = np.ascontiguousarray(np.asarray(L0_attn_W_rec, dtype=np.float32).T)
    wv = np.ascontiguousarray(np.asarray(L0_attn_W_val, dtype=np.float32)[:, _PERM].T)
    wg = np.ascontiguousarray(np.asarray(L0_ffn_w_gate, dtype=np.float32).T)
    wu = np.ascontiguousarray(np.asarray(L0_ffn_w_up, dtype=np.float32).T)
    wd = np.asarray(L0_ffn_w_down, dtype=np.float32)
    idx = np.asarray(L0_ffn_down_reorder_index)
    wde = np.empty_like(wd)
    wde[:, idx] = wd
    wdT = np.ascontiguousarray(wde.T)
    return wa, wr, wv, wg, wu, wdT


def kernel(L0_attn_x, L0_attn_W_act, L0_attn_W_val, L0_attn_W_rec,
           L0_ffn_w_gate, L0_ffn_w_up, L0_ffn_w_down,
           L0_ffn_up_reorder_index, L0_ffn_down_reorder_index):
    x = np.asarray(L0_attn_x, dtype=np.float32)
    B = x.shape[0]
    wa, wr, wv, wg, wu, wdT = _prep_weights(L0_attn_W_act, L0_attn_W_val, L0_attn_W_rec,
                                            L0_ffn_w_gate, L0_ffn_w_up, L0_ffn_w_down,
                                            L0_ffn_down_reorder_index)
    ident = np.eye(128, dtype=np.float32)
    nc = build_program(PHASES)
    in_maps = []
    for b in range(B):
        in_maps.append({
            "xT": np.ascontiguousarray(x[b].T),
            "waT": wa, "wrT": wr, "wvT": wv, "wgT": wg, "wuT": wu, "wdT": wdT,
            "ident": ident,
        })
    res = run_bass_kernel_spmd(nc, in_maps, core_ids=list(range(B)))
    out = np.stack([np.ascontiguousarray(r["outT"].T) for r in res.results], axis=0)
    return out


# version 05
# speedup vs baseline: 1.2242x; 1.0112x over previous
import numpy as np
import contextlib
import concourse.bass as bass
import concourse.mybir as mybir
from concourse.bass_utils import run_bass_kernel_spmd

F32 = mybir.dt.float32
BF16 = mybir.dt.bfloat16
FP8 = mybir.dt.float8e4
AF = mybir.ActivationFunctionType
ALU = mybir.AluOpType

C = 2048
HC = 16
I = 5632
IC = 44
T = 2048
TT = 512
NTILE = 4
XSTR = 2056
MAGIC = 1.5 * (2.0 ** 23)
INV2PI = 1.0 / (2.0 * np.pi)
TWO_PI = 2.0 * np.pi

PHASES = ("ffn1", "attn", "ffn2")
DBG_FFN1 = False


class _Sched:

    ENGINES = ("tensor", "vector", "scalar", "gpsimd", "sync")

    def __init__(self):
        self.items = []

    def emit(self, engine, fn, after=(), dma=False, force=False, group=None):
        assert engine in self.ENGINES, engine
        if dma:
            assert group is not None, "every DMA needs an explicit group"
        sem = ("g:" + group) if dma else engine
        self.items.append(dict(engine=engine, after=[a for a in after if a is not None],
                               fn=fn, amt=16 if dma else 1, dma=dma, force=force, sem=sem))
        return len(self.items) - 1

    def build(self, nc, exit_engine="sync", exit_waits=()):
        needed = set(exit_waits)
        for i, it in enumerate(self.items):
            if it["force"]:
                needed.add(i)
            for a in it["after"]:
                src = self.items[a]
                if src["dma"] or src["engine"] != it["engine"]:
                    needed.add(a)
        val = {}
        counts = {}
        for i, it in enumerate(self.items):
            if i in needed:
                counts[it["sem"]] = counts.get(it["sem"], 0) + it["amt"]
                val[i] = counts[it["sem"]]
        used_sems = sorted(set(counts) | {exit_engine})
        used_eng = sorted({it["engine"] for it in self.items} | {exit_engine})
        stack = contextlib.ExitStack()
        sems = {nm: stack.enter_context(nc.semaphore(f"sem_{nm.replace(':', '_')}")) for nm in used_sems}
        block = stack.enter_context(nc.Block())

        def section(eng_name):
            def body(eng):
                last_wait = {}
                for i, it in enumerate(self.items):
                    if it["engine"] != eng_name:
                        continue
                    waits = {}
                    for a in it["after"]:
                        src = self.items[a]
                        if src["engine"] == eng_name and not src["dma"]:
                            continue
                        v = val.get(a)
                        if v is None:
                            raise RuntimeError(f"item {i} waits un-sem'd item {a}")
                        nm = src["sem"]
                        if v > waits.get(nm, 0):
                            waits[nm] = v
                    for nm, v in waits.items():
                        if v > last_wait.get(nm, 0):
                            eng.wait_ge(sems[nm], v)
                            last_wait[nm] = v
                    inst = it["fn"](eng)
                    if i in needed:
                        if inst is None:
                            raise RuntimeError(f"needed item {i} ({eng_name}) returned no instruction")
                        inst.then_inc(sems[it["sem"]], it["amt"])
                if eng_name == exit_engine:
                    for nm in used_sems:
                        c = counts.get(nm, 0)
                        if c > 0 and c > last_wait.get(nm, 0):
                            eng.wait_ge(sems[nm], c)
            return body

        for e in used_eng:
            getattr(block, e)(section(e))
        stack.close()


def build_program(phases=PHASES):
    nc = bass.Bass("TRN2", target_bir_lowering=False, debug=False)
    s = _Sched()
    gstack = contextlib.ExitStack()

    xT = nc.dram_tensor("xT", [C, T], F32, kind="ExternalInput").ap()
    waT = nc.dram_tensor("waT", [2 * C, C], F32, kind="ExternalInput").ap()
    wrT = nc.dram_tensor("wrT", [2 * C, C], F32, kind="ExternalInput").ap()
    wvT = nc.dram_tensor("wvT", [C, C], F32, kind="ExternalInput").ap()
    wgT = nc.dram_tensor("wgT", [C, I], F32, kind="ExternalInput").ap()
    wuT = nc.dram_tensor("wuT", [C, I], F32, kind="ExternalInput").ap()
    wdT = nc.dram_tensor("wdT", [I, C], F32, kind="ExternalInput").ap()
    ident = nc.dram_tensor("ident", [128, 128], F32, kind="ExternalInput").ap()
    outT = nc.dram_tensor("outT", [C, T], F32, kind="ExternalOutput").ap()

    resid = nc.dram_tensor("resid", [C, T], F32, kind="Internal").ap()
    wgq = nc.dram_tensor("wgq", [IC, 128, HC * 128], FP8, kind="Internal").ap()
    wuq = nc.dram_tensor("wuq", [IC, 128, HC * 128], FP8, kind="Internal").ap()
    wdq = nc.dram_tensor("wdq", [HC, 128, IC * 128], FP8, kind="Internal").ap()

    xT_r = xT.rearrange("(hc p) t -> p hc t", p=128)
    resid_r = resid.rearrange("(hc p) t -> p hc t", p=128)
    outT_r = outT.rearrange("(hc p) t -> p hc t", p=128)
    wgT_r = wgT.rearrange("(jc p) i -> p jc i", p=128)
    wuT_r = wuT.rearrange("(jc p) i -> p jc i", p=128)
    wdT_r = wdT.rearrange("(icb p) h -> p icb h", p=128)
    waT_r = waT.rearrange("(kc p) c -> p kc c", p=128)
    wrT_r = wrT.rearrange("(kc p) c -> p kc c", p=128)
    wvT_r = wvT.rearrange("(kc p) c -> p kc c", p=128)

    sb = lambda name, shape, dt: gstack.enter_context(nc.sbuf_tensor(name, shape, dt))
    idt = sb("idt", [128, 128], F32)
    ones1 = sb("ones1", [1, 128], F32)
    ones128 = sb("ones128", [128, 128], BF16)
    wsg = sb("wsg", [128, IC], F32)
    wsu = sb("wsu", [128, IC], F32)
    wsd = sb("wsd", [128, HC], F32)

    t_ident = s.emit("sync", lambda e: e.dma_start(out=idt[:], in_=ident[:, :]), dma=True, group="init")
    t_ones1 = s.emit("vector", lambda e: e.memset(ones1[:], 1.0))
    t_ones128 = s.emit("vector", lambda e: e.memset(ones128[:], 1.0))
    init_toks = [t_ident, t_ones1, t_ones128]

    qstack = contextlib.ExitStack()
    qsb = lambda name, shape, dt: qstack.enter_context(nc.sbuf_tensor(name, shape, dt))
    psq = qstack.enter_context(nc.psum_tensor("psq", [128, 128], F32))
    QH = IC // 4
    wraw = qsb("q_wraw", [128, HC * 128], F32)
    wdraw = qsb("q_wdraw", [128, QH * 128], F32)
    q8g = [qsb("q_q8g0", [128, HC * 128], FP8)] * 2
    q8d = qsb("q_q8d", [128, QH * 128], FP8)
    Mxq = qsb("q_Mx", [128, 128], F32)
    colsq = qsb("q_cols", [128, 2], F32)
    rowq = qsb("q_row", [1, 128], F32)

    cache_g = [None] * IC
    cache_u = [None] * IC
    cache_d = [None] * HC

    def q_scale_chain(t_reduce_src, after):
        tt = s.emit("tensor", lambda e: e.transpose(psq[:, 0:128], Mxq[:, 0:128], idt[:]),
                    after=after + [t_ident])
        tr = s.emit("vector", lambda e: e.tensor_reduce(out=colsq[:, 0:1], in_=psq[:, 0:128],
                                                        axis=mybir.AxisListType.X, op=ALU.max), after=[tt])
        t_t2 = s.emit("tensor", lambda e: e.transpose(psq[0:1, 0:128], colsq[:, 0:1], idt[:]),
                      after=[tr])
        t_cr = s.emit("vector", lambda e: e.tensor_copy(out=rowq[0:1, 0:128], in_=psq[0:1, 0:128]), after=[t_t2])
        t_ln = s.emit("scalar", lambda e: e.activation(out=rowq[0:1, 0:128], in_=rowq[0:1, 0:128],
                                                       func=AF.Ln), after=[t_cr])
        t_ex = s.emit("scalar", lambda e: e.activation(out=rowq[0:1, 0:128], in_=rowq[0:1, 0:128],
                                                       func=AF.Exp, scale=-1.0), after=[t_ln])
        t_um = s.emit("vector", lambda e: e.tensor_scalar(out=rowq[0:1, 0:128], in0=rowq[0:1, 0:128],
                                                          scalar1=224.0, scalar2=None, op0=ALU.mult), after=[t_ex])
        t_bc = s.emit("tensor", lambda e: e.matmul(psq[:, 0:128], ones1[0:1, :], rowq[0:1, 0:128],
                                                   start=True, stop=True), after=[t_um, t_ones1])
        return tr, t_bc

    prev_q = [None]
    prev_cw = [None, None]
    prev_ps = [None]

    def quant_gateup(which, ic, wT_r, cache, qbuf_par):
        qbuf_par = 0
        tl = s.emit("sync", lambda e: e.dma_start(out=wraw[:].rearrange("p (jc i) -> p jc i", jc=HC),
                                                  in_=wT_r[:, :, ic * 128:(ic + 1) * 128]),
                    after=[prev_q[0]] if prev_q[0] is not None else list(init_toks), dma=True, group="ql")
        v_in = wraw[:].rearrange("p (jc i) -> p i jc", jc=HC)
        tm = s.emit("vector", lambda e: e.tensor_reduce(out=Mxq[:, 0:128], in_=v_in, axis=mybir.AxisListType.X,
                                                        op=ALU.max, apply_absolute_value=True),
                    after=[tl, prev_ps[0]])
        tr, t_bc = q_scale_chain(tm, after=[tm])
        sc_t = wsg if which == "g" else wsu
        tsc = s.emit("vector", lambda e: e.tensor_scalar(out=sc_t[:, ic:ic + 1], in0=colsq[:, 0:1],
                                                         scalar1=1.0 / 224.0, scalar2=None, op0=ALU.mult), after=[tr])
        qb = q8g[qbuf_par]
        qts = []
        for jc in range(HC):
            qts.append(s.emit("vector", lambda e, jc=jc: e.tensor_tensor(
                out=qb[:, jc * 128:(jc + 1) * 128], in0=wraw[:, jc * 128:(jc + 1) * 128],
                in1=psq[:, 0:128], op=ALU.mult),
                after=[t_bc] + ([prev_cw[qbuf_par]] if jc == 0 else [qts[-1]])))
        t_cw = s.emit("sync", lambda e, ic=ic: e.dma_start(out=cache[ic], in_=qb[:]), after=[qts[-1]], dma=True, group="qcg")
        prev_q[0] = qts[-1]
        prev_ps[0] = qts[-1]
        prev_cw[qbuf_par] = t_cw
        return t_cw

    prev_dq = [None]
    prev_dcw = [None]

    def quant_down2(hcc):
        t_prev = None
        for q in range(4):
            t_l = s.emit("sync", lambda e, q=q: e.dma_start(
                out=wdraw[:].rearrange("p (icb hh) -> p icb hh", icb=QH),
                in_=wdT_r[:, q * QH:(q + 1) * QH, hcc * 128:(hcc + 1) * 128]),
                after=[t_prev if t_prev is not None else (prev_dq[0] if prev_dq[0] is not None else prev_q[0])],
                dma=True, group="qd")
            v_in = wdraw[:].rearrange("p (icb hh) -> p hh icb", icb=QH)
            if q == 0:
                t_m = s.emit("vector", lambda e: e.tensor_reduce(out=Mxq[:, 0:128], in_=v_in,
                                                                 axis=mybir.AxisListType.X, op=ALU.max,
                                                                 apply_absolute_value=True),
                             after=[t_l, prev_ps[0]])
            else:
                t_r2 = s.emit("vector", lambda e: e.tensor_reduce(out=psq_scr[:, 0:128], in_=v_in,
                                                                  axis=mybir.AxisListType.X, op=ALU.max,
                                                                  apply_absolute_value=True), after=[t_l])
                t_m = s.emit("vector", lambda e: e.tensor_tensor(out=Mxq[:, 0:128], in0=Mxq[:, 0:128],
                                                                 in1=psq_scr[:, 0:128], op=ALU.max), after=[t_r2, t_m])
            t_prev = t_m
        tr, t_bc = q_scale_chain(t_prev, after=[t_prev])
        s.emit("vector", lambda e: e.tensor_scalar(out=wsd[:, hcc:hcc + 1], in0=colsq[:, 0:1],
                                                   scalar1=0.5 / 224.0, scalar2=None, op0=ALU.mult), after=[tr])
        cws = []
        t_data = t_prev
        for q in range(4):
            t_data = s.emit("sync", lambda e, q=q: e.dma_start(
                out=wdraw[:].rearrange("p (icb hh) -> p icb hh", icb=QH),
                in_=wdT_r[:, q * QH:(q + 1) * QH, hcc * 128:(hcc + 1) * 128]),
                after=[t_data, t_bc], dma=True, group="qd")
            qts = []
            for icb in range(QH):
                qts.append(s.emit("vector", lambda e, icb=icb: e.tensor_tensor(
                    out=q8d[:, icb * 128:(icb + 1) * 128], in0=wdraw[:, icb * 128:(icb + 1) * 128],
                    in1=psq[:, 0:128], op=ALU.mult),
                    after=[t_data, t_bc] + ([prev_dcw[0]] if icb == 0 else [qts[-1]])))
            t_cw = s.emit("sync", lambda e, hcc=hcc, q=q: e.dma_start(
                out=wdq[hcc][:, q * QH * 128:(q + 1) * QH * 128], in_=q8d[:]), after=[qts[-1]], dma=True, group="qcd")
            prev_dcw[0] = t_cw
            cws.append(t_cw)
            t_data = qts[-1]
            prev_dq[0] = qts[-1]
            prev_ps[0] = qts[-1]
        return cws

    psq_scr = qsb("q_scr", [128, 128], F32)

    use_ffn = ("ffn1" in phases) or ("ffn2" in phases)
    if use_ffn:
        for ic in range(IC):
            cache_g[ic] = quant_gateup("g", ic, wgT_r, wgq, 0)
            cache_u[ic] = quant_gateup("u", ic, wuT_r, wuq, 1)
        for hcc in range(HC):
            cache_d[hcc] = quant_down2(hcc)

    def ffn(src_r, dst_r, prev_tail, tag):
        ph = contextlib.ExitStack()
        psb = lambda name, shape: ph.enter_context(nc.psum_tensor(name, shape, F32))
        ps_sum = psb(f"{tag}_ps_sum", [128, TT])
        ps_sq = psb(f"{tag}_ps_sq", [128, TT])
        ps_bc = psb(f"{tag}_ps_bc", [128, TT])
        ps_g = [psb(f"{tag}_ps_g0", [128, TT]), psb(f"{tag}_ps_g1", [128, TT])]
        ps_u = [psb(f"{tag}_ps_u0", [128, TT]), psb(f"{tag}_ps_u1", [128, TT])]
        ps_dn = [ps_sq, ps_sum]

        xin = ph.enter_context(nc.sbuf_tensor(f"{tag}_xin", [128, HC * TT], F32))
        xbb = [ph.enter_context(nc.sbuf_tensor(f"{tag}_xbb{k}", [128, TT], BF16)) for k in range(2)]
        qx = ph.enter_context(nc.sbuf_tensor(f"{tag}_qx", [128, HC * TT], FP8))
        t0 = ph.enter_context(nc.sbuf_tensor(f"{tag}_t0", [128, IC * TT], F32))
        qt = ph.enter_context(nc.sbuf_tensor(f"{tag}_qt", [128, IC * TT], FP8))
        scr1 = ph.enter_context(nc.sbuf_tensor(f"{tag}_scr1", [128, TT], F32))
        scr2 = ph.enter_context(nc.sbuf_tensor(f"{tag}_scr2", [128, TT], F32))
        Mx = ph.enter_context(nc.sbuf_tensor(f"{tag}_Mx", [128, TT], F32))
        cols = ph.enter_context(nc.sbuf_tensor(f"{tag}_cols", [128, 8], F32))
        rows = ph.enter_context(nc.sbuf_tensor(f"{tag}_rows", [1, 2 * TT], F32))
        bsc = ph.enter_context(nc.sbuf_tensor(f"{tag}_bsc", [128, TT], F32))
        qwg = [ph.enter_context(nc.sbuf_tensor(f"{tag}_qwg{k}", [128, HC * 128], FP8)) for k in range(2)]
        qwu = [ph.enter_context(nc.sbuf_tensor(f"{tag}_qwu{k}", [128, HC * 128], FP8)) for k in range(2)]
        qwd = [ph.enter_context(nc.sbuf_tensor(f"{tag}_qwd{k}", [128, IC * 128], FP8)) for k in range(2)]

        tail = list(prev_tail)

        def maxcol_bcast(after, extra_row_fn, consume_fn):
            nb = TT // 128
            toks = []
            for b in range(nb):
                tt = s.emit("tensor", lambda e, b=b: e.transpose(ps_bc[:, 0:128], Mx[:, b * 128:(b + 1) * 128], idt[:]),
                            after=after + [t_ident] + ([toks[-1]] if toks else []))
                rd = s.emit("vector", lambda e, b=b: e.tensor_reduce(out=cols[:, b:b + 1], in_=ps_bc[:, 0:128],
                                                                     axis=mybir.AxisListType.X, op=ALU.max),
                            after=[tt])
                toks.append(rd)
            row_toks = []
            for b in range(nb):
                t1 = s.emit("tensor", lambda e, b=b: e.transpose(ps_bc[0:1, 0:128], cols[:, b:b + 1], idt[:]),
                            after=[toks[-1]] + ([row_toks[-1]] if row_toks else []))
                c1 = s.emit("vector", lambda e, b=b: e.tensor_copy(out=rows[0:1, b * 128:(b + 1) * 128], in_=ps_bc[0:1, 0:128]),
                            after=[t1])
                row_toks.append(c1)
            t_ln = s.emit("scalar", lambda e: e.activation(out=rows[0:1, TT:2 * TT], in_=rows[0:1, 0:TT],
                                                           func=AF.Ln), after=row_toks)
            t_ex = s.emit("scalar", lambda e: e.activation(out=rows[0:1, TT:2 * TT], in_=rows[0:1, TT:2 * TT],
                                                           func=AF.Exp, scale=-1.0), after=[t_ln])
            t_uc = s.emit("vector", lambda e: e.tensor_scalar(out=rows[0:1, TT:2 * TT], in0=rows[0:1, TT:2 * TT],
                                                              scalar1=224.0, scalar2=None, op0=ALU.mult), after=[t_ex])
            tb = s.emit("tensor", lambda e: e.matmul(ps_bc[:, 0:TT], ones1[0:1, :], rows[0:1, TT:2 * TT], start=True, stop=True),
                        after=[t_uc, t_ones1])
            t_cons = consume_fn(tb)
            r_last = extra_row_fn([row_toks[-1], tb])
            tb2 = s.emit("tensor", lambda e: e.matmul(ps_bc[:, 0:TT], ones1[0:1, :], rows[0:1, TT:2 * TT], start=True, stop=True),
                         after=[r_last, t_cons])
            t_bsc = s.emit("vector", lambda e: e.tensor_copy(out=bsc[:, 0:TT], in_=ps_bc[:, 0:TT]), after=[tb2])
            return t_cons, t_bsc

        guard0 = prev_tail[-1] if prev_tail else None
        wg_rd = [guard0, guard0]
        wu_rd = [guard0, guard0]
        wd_rd = [guard0, guard0]
        eplg1 = [None, None]
        eplg3 = [None, None]
        d_ep1 = [None, None]
        st_prev = [None, None]

        for tile_i in range(NTILE):
            tk0 = tile_i * TT
            t_x = s.emit("sync", lambda e, tk0=tk0: e.dma_start(
                out=xin[:].rearrange("p (hc t) -> p hc t", hc=HC),
                in_=src_r[:, :, tk0:tk0 + TT]), after=tail, dma=True, group=tag + "x")
            mm_s = []
            mm_q = []
            for hc in range(HC):
                par = hc % 2
                tc = s.emit("scalar", lambda e, hc=hc, par=par: e.activation(
                    out=xbb[par][:], in_=xin[:, hc * TT:(hc + 1) * TT], func=AF.Copy),
                    after=[t_x] + ([mm_q[-2]] if len(mm_q) >= 2 else []))
                ms = s.emit("tensor", lambda e, hc=hc, par=par: e.matmul(
                    ps_sum[:, 0:TT], ones128[:], xbb[par][:], start=(hc == 0), stop=(hc == HC - 1)),
                    after=[tc, t_ones128])
                tq = s.emit("scalar", lambda e, hc=hc, par=par: e.activation(
                    out=xbb[par][:], in_=xin[:, hc * TT:(hc + 1) * TT], func=AF.Square),
                    after=[ms])
                mq = s.emit("tensor", lambda e, hc=hc, par=par: e.matmul(
                    ps_sq[:, 0:TT], ones128[:], xbb[par][:], start=(hc == 0), stop=(hc == HC - 1)),
                    after=[tq])
                mm_s.append(ms)
                mm_q.append(mq)
            t_sum = mm_s[-1]
            t_sqm = mm_q[-1]
            tree = []
            for hc in range(HC):
                td = s.emit("vector", lambda e, hc=hc: e.scalar_tensor_tensor(
                    out=scr1[:, 0:TT], in0=ps_sum[:, 0:TT], scalar=-1.0 / 2048.0,
                    in1=xin[:, hc * TT:(hc + 1) * TT], op0=ALU.mult, op1=ALU.add),
                    after=[t_sum] + ([tree[-1]] if tree else []))
                ta = s.emit("scalar", lambda e: e.activation(out=scr2[:, 0:TT], in_=scr1[:, 0:TT], func=AF.Abs),
                            after=[td])
                if hc == 0:
                    tm = s.emit("vector", lambda e: e.tensor_copy(out=Mx[:, 0:TT], in_=scr2[:, 0:TT]), after=[ta])
                else:
                    tm = s.emit("vector", lambda e: e.tensor_tensor(out=Mx[:, 0:TT], in0=Mx[:, 0:TT],
                                                                    in1=scr2[:, 0:TT], op=ALU.max), after=[ta])
                tree.append(tm)

            def xrow_fn(aft):
                r1 = s.emit("vector", lambda e: e.tensor_scalar(out=rows[0:1, TT:2 * TT], in0=ps_sum[0:1, 0:TT],
                                                                scalar1=1.0 / 2048.0, scalar2=None, op0=ALU.mult),
                            after=[t_sum] + aft)
                r2 = s.emit("vector", lambda e: e.tensor_tensor(out=rows[0:1, TT:2 * TT], in0=rows[0:1, TT:2 * TT],
                                                                in1=rows[0:1, TT:2 * TT], op=ALU.mult), after=[r1])
                r3 = s.emit("vector", lambda e: e.scalar_tensor_tensor(out=rows[0:1, TT:2 * TT], in0=ps_sq[0:1, 0:TT],
                                                                       scalar=1.0 / 2048.0, in1=rows[0:1, TT:2 * TT],
                                                                       op0=ALU.mult, op1=ALU.subtract),
                            after=[r2, t_sqm])
                r4 = s.emit("vector", lambda e: e.tensor_scalar(out=rows[0:1, TT:2 * TT], in0=rows[0:1, TT:2 * TT],
                                                                scalar1=1e-5, scalar2=None, op0=ALU.add), after=[r3])
                r5 = s.emit("scalar", lambda e: e.activation(out=rows[0:1, TT:2 * TT], in_=rows[0:1, TT:2 * TT], func=AF.Ln), after=[r4])
                r6 = s.emit("scalar", lambda e: e.activation(out=rows[0:1, TT:2 * TT], in_=rows[0:1, TT:2 * TT], func=AF.Exp,
                                                             scale=-0.5), after=[r5])
                r7 = s.emit("vector", lambda e: e.tensor_tensor(out=rows[0:1, TT:2 * TT], in0=rows[0:1, 0:TT],
                                                                in1=rows[0:1, TT:2 * TT], op=ALU.mult), after=[r6])
                r8 = s.emit("vector", lambda e: e.tensor_scalar(out=rows[0:1, TT:2 * TT], in0=rows[0:1, TT:2 * TT],
                                                                scalar1=1.0 / 224.0, scalar2=None, op0=ALU.mult), after=[r7])
                return r8

            def qx_consume(tb):
                toks = []
                for hc in range(HC):
                    td = s.emit("vector", lambda e, hc=hc: e.scalar_tensor_tensor(
                        out=scr1[:, 0:TT], in0=ps_sum[:, 0:TT], scalar=-1.0 / 2048.0,
                        in1=xin[:, hc * TT:(hc + 1) * TT], op0=ALU.mult, op1=ALU.add),
                        after=[tb] + ([toks[-1]] if toks else []))
                    tqx = s.emit("vector", lambda e, hc=hc: e.tensor_tensor(
                        out=qx[:, hc * TT:(hc + 1) * TT], in0=scr1[:, 0:TT], in1=ps_bc[:, 0:TT], op=ALU.mult),
                        after=[td])
                    toks.append(tqx)
                return toks[-1]

            t_qx, tb_bxs = maxcol_bcast(after=[tree[-1]], extra_row_fn=xrow_fn, consume_fn=qx_consume)

            tMt = None
            for ic in range(IC):
                par = ic % 2
                wg_tok = s.emit("sync", lambda e, ic=ic, par=par: e.dma_start(out=qwg[par][:], in_=wgq[ic]),
                                after=[cache_g[ic], wg_rd[par]], dma=True, group=f"{tag}g{par}")
                wu_tok = s.emit("sync", lambda e, ic=ic, par=par: e.dma_start(out=qwu[par][:], in_=wuq[ic]),
                                after=[cache_u[ic], wu_rd[par]], dma=True, group=f"{tag}u{par}")
                mmg = []
                for jc2 in range(HC // 2):
                    mmg.append(s.emit("tensor", lambda e, jc2=jc2, par=par: e.matmul(
                        ps_g[par][:, 0:TT],
                        qwg[par][:, jc2 * 256:(jc2 + 1) * 256].rearrange("p (i m) -> p i m", i=2),
                        qx[:, jc2 * 2 * TT:(jc2 + 1) * 2 * TT].rearrange("p (i n) -> p i n", i=2),
                        start=(jc2 == 0), stop=(jc2 == HC // 2 - 1),
                        perf_mode=mybir.MatmulPerfMode.DoubleRow),
                        after=[t_qx, wg_tok, eplg1[par]] + ([mmg[-1]] if mmg else [])))
                mmu = []
                for jc2 in range(HC // 2):
                    mmu.append(s.emit("tensor", lambda e, jc2=jc2, par=par: e.matmul(
                        ps_u[par][:, 0:TT],
                        qwu[par][:, jc2 * 256:(jc2 + 1) * 256].rearrange("p (i m) -> p i m", i=2),
                        qx[:, jc2 * 2 * TT:(jc2 + 1) * 2 * TT].rearrange("p (i n) -> p i n", i=2),
                        start=(jc2 == 0), stop=(jc2 == HC // 2 - 1),
                        perf_mode=mybir.MatmulPerfMode.DoubleRow),
                        after=[wu_tok, eplg3[par], mmg[-1]] + ([mmu[-1]] if mmu else [])))
                wg_rd[par] = mmg[-1]
                wu_rd[par] = mmu[-1]
                e1 = s.emit("vector", lambda e, ic=ic, par=par: e.scalar_tensor_tensor(
                    out=scr1[:, 0:TT], in0=ps_g[par][:, 0:TT], scalar=wsg[:, ic:ic + 1], in1=bsc[:, 0:TT],
                    op0=ALU.mult, op1=ALU.mult), after=[mmg[-1], tb_bxs])
                e2 = s.emit("scalar", lambda e: e.activation(out=scr1[:, 0:TT], in_=scr1[:, 0:TT], func=AF.Silu),
                            after=[e1])
                e3 = s.emit("vector", lambda e, ic=ic, par=par: e.scalar_tensor_tensor(
                    out=scr2[:, 0:TT], in0=ps_u[par][:, 0:TT], scalar=wsu[:, ic:ic + 1], in1=bsc[:, 0:TT],
                    op0=ALU.mult, op1=ALU.mult), after=[mmu[-1], e1])
                e4 = s.emit("vector", lambda e, ic=ic: e.tensor_tensor(
                    out=t0[:, ic * TT:(ic + 1) * TT], in0=scr1[:, 0:TT], in1=scr2[:, 0:TT], op=ALU.mult),
                    after=[e2, e3])
                eplg1[par] = e1
                eplg3[par] = e3
                ta = s.emit("scalar", lambda e, ic=ic: e.activation(out=scr2[:, 0:TT],
                                                                    in_=t0[:, ic * TT:(ic + 1) * TT], func=AF.Abs),
                            after=[e4])
                if ic == 0:
                    tMt = s.emit("vector", lambda e: e.tensor_copy(out=Mx[:, 0:TT], in_=scr2[:, 0:TT]),
                                 after=[ta, tb_bxs])
                else:
                    tMt = s.emit("vector", lambda e: e.tensor_tensor(out=Mx[:, 0:TT], in0=Mx[:, 0:TT],
                                                                     in1=scr2[:, 0:TT], op=ALU.max), after=[ta, tMt])

            def trow_fn(aft):
                r8 = s.emit("vector", lambda e: e.tensor_scalar(out=rows[0:1, TT:2 * TT], in0=rows[0:1, 0:TT],
                                                                scalar1=1.0 / 224.0, scalar2=None, op0=ALU.mult),
                            after=aft)
                return r8

            def qt_consume(tb):
                toks = []
                for ic in range(IC):
                    toks.append(s.emit("vector", lambda e, ic=ic: e.tensor_tensor(
                        out=qt[:, ic * TT:(ic + 1) * TT], in0=t0[:, ic * TT:(ic + 1) * TT],
                        in1=ps_bc[:, 0:TT], op=ALU.mult),
                        after=[tb] + ([toks[-1]] if toks else [])))
                return toks[-1]

            t_qt, tb_bts = maxcol_bcast(after=[tMt], extra_row_fn=trow_fn, consume_fn=qt_consume)

            st_toks = []
            for hcc in range(HC):
                par = hcc % 2
                wd_tok = s.emit("sync", lambda e, hcc=hcc, par=par: e.dma_start(out=qwd[par][:], in_=wdq[hcc]),
                                after=(cache_d[hcc] if isinstance(cache_d[hcc], list) else [cache_d[hcc]]) + [wd_rd[par]],
                                dma=True, group=f"{tag}d{par}")
                mmd = []
                for icb2 in range(IC // 2):
                    mmd.append(s.emit("tensor", lambda e, icb2=icb2, par=par: e.matmul(
                        ps_dn[par][:, 0:TT],
                        qwd[par][:, icb2 * 256:(icb2 + 1) * 256].rearrange("p (i m) -> p i m", i=2),
                        qt[:, icb2 * 2 * TT:(icb2 + 1) * 2 * TT].rearrange("p (i n) -> p i n", i=2),
                        start=(icb2 == 0), stop=(icb2 == IC // 2 - 1),
                        perf_mode=mybir.MatmulPerfMode.DoubleRow),
                        after=[t_qt, wd_tok, d_ep1[par]] + ([mmd[-1]] if mmd else [])))
                wd_rd[par] = mmd[-1]
                ep1 = s.emit("vector", lambda e, hcc=hcc, par=par: e.scalar_tensor_tensor(
                    out=scr1[:, 0:TT], in0=ps_dn[par][:, 0:TT], scalar=wsd[:, hcc:hcc + 1], in1=bsc[:, 0:TT],
                    op0=ALU.mult, op1=ALU.mult),
                    after=[mmd[-1], tb_bts])
                ep2 = s.emit("vector", lambda e, hcc=hcc: e.tensor_tensor(
                    out=xin[:, hcc * TT:(hcc + 1) * TT], in0=scr1[:, 0:TT],
                    in1=xin[:, hcc * TT:(hcc + 1) * TT], op=ALU.add),
                    after=[ep1])
                d_ep1[par] = ep1
                st = s.emit("scalar", lambda e, hcc=hcc, tk0=tk0: e.dma_start(
                    out=dst_r[:, hcc, tk0:tk0 + TT], in_=xin[:, hcc * TT:(hcc + 1) * TT]),
                    after=[ep2], dma=True, force=True, group=tag + "s")
                st_toks.append(st)
            tail = [st_toks[-1]]

        if DBG_FFN1 and tag == "f1":
            def dbg_out(name, ap, dt=F32, cast=False):
                od = nc.dram_tensor("dbg_" + name, list(ap.shape), dt, kind="ExternalOutput").ap()
                eng = "gpsimd" if cast else "sync"
                s.emit(eng, lambda e, od=od, ap=ap: e.dma_start(out=od, in_=ap),
                       after=list(tail), dma=True, force=True, group="dbg")
            dbg_out("wsg", wsg[:])
            dbg_out("wsu", wsu[:])
            dbg_out("wsd", wsd[:])
            dbg_out("rows", rows[0:1, :])
            dbg_out("bsc", bsc[:])
            dbg_out("Mx", Mx[:])
            dbg_out("qx", qx[:], cast=True)
            dbg_out("qwg0", qwg[0][:], cast=True)
            dbg_out("qwd0", qwd[0][:], cast=True)
            dbg_out("wgq0", wgq[0], cast=True)
            dbg_out("wgq40", wgq[40], cast=True)
            dbg_out("wdq0", wdq[0], cast=True)
            dbg_out("t0a", t0[:, 0:4 * TT])
            dbg_out("qta", qt[:, 0:4 * TT], cast=True)

        ph.close()
        return tail

    def attention(prev_tail):
        ph = contextlib.ExitStack()
        psb = lambda name, shape: ph.enter_context(nc.psum_tensor(name, shape, F32))
        ps_big = [psb("a_big0", [128, 2048]), psb("a_big1", [128, 2048])]
        xn = ph.enter_context(nc.sbuf_tensor("a_xn", [128, HC * XSTR], BF16))
        kv = ph.enter_context(nc.sbuf_tensor("a_kv", [128, HC * 2048], BF16))
        pool = [ph.enter_context(nc.sbuf_tensor(f"a_pl{k}", [128, 2048], F32)) for k in range(5)]
        xa = [ph.enter_context(nc.sbuf_tensor(f"a_xa{k}", [128, 512], F32)) for k in range(2)]
        xbb0 = ph.enter_context(nc.sbuf_tensor("a_xbb0", [128, 512], BF16))
        xbb = [xbb0, xbb0]
        scr = ph.enter_context(nc.sbuf_tensor("a_scr", [128, 512], F32))
        rsc = ph.enter_context(nc.sbuf_tensor("a_rsc", [1, 1024], F32))
        binv = ph.enter_context(nc.sbuf_tensor("a_binv", [128, 512], F32))
        yst = [scr, binv]
        wa = [ph.enter_context(nc.sbuf_tensor(f"a_wa{k}", [128, 2 * HC * 128], BF16)) for k in range(2)]
        wv = [ph.enter_context(nc.sbuf_tensor(f"a_wv{k}", [128, HC * 128], BF16)) for k in range(2)]

        tail = list(prev_tail)

        t_z = None
        for hc in range(HC):
            t_z = s.emit("vector", lambda e, hc=hc: e.memset(xn[:, hc * XSTR:hc * XSTR + 1], 0.0), after=tail)

        ln_prev = None
        xa_rd = [None, None]
        for tb in range(4):
            tk0 = tb * 512
            mm_s = []
            mm_q = []
            for hc in range(HC):
                par = hc % 2
                t_l = s.emit("sync", lambda e, hc=hc, par=par, tk0=tk0: e.dma_start(
                    out=xa[par][:], in_=resid_r[:, hc, tk0:tk0 + 512]),
                    after=([xa_rd[par]] if xa_rd[par] is not None else tail) + ([ln_prev] if (hc < 2 and ln_prev) else []),
                    dma=True, group=f"ax{par}")
                tc = s.emit("scalar", lambda e, par=par: e.activation(out=xbb[par][:], in_=xa[par][:], func=AF.Copy),
                            after=[t_l] + ([mm_q[-1]] if mm_q else []))
                ms = s.emit("tensor", lambda e, hc=hc, par=par: e.matmul(
                    ps_big[0][:, 0:512], ones128[:], xbb[par][:], start=(hc == 0), stop=(hc == HC - 1)),
                    after=[tc, t_ones128] + ([ln_prev] if (hc == 0 and ln_prev) else []))
                tq = s.emit("scalar", lambda e, par=par: e.activation(out=xbb[par][:], in_=xa[par][:], func=AF.Square),
                            after=[ms])
                mq = s.emit("tensor", lambda e, hc=hc, par=par: e.matmul(
                    ps_big[0][:, 512:1024], ones128[:], xbb[par][:], start=(hc == 0), stop=(hc == HC - 1)),
                    after=[tq])
                mm_s.append(ms)
                mm_q.append(mq)
                xa_rd[par] = tq
            r1 = s.emit("vector", lambda e: e.tensor_scalar(out=rsc[0:1, 0:512], in0=ps_big[0][0:1, 0:512],
                                                            scalar1=1.0 / 2048.0, scalar2=None, op0=ALU.mult),
                        after=[mm_s[-1]])
            r2 = s.emit("vector", lambda e: e.tensor_tensor(out=rsc[0:1, 512:1024], in0=rsc[0:1, 0:512],
                                                            in1=rsc[0:1, 0:512], op=ALU.mult), after=[r1])
            r3 = s.emit("vector", lambda e: e.scalar_tensor_tensor(out=rsc[0:1, 512:1024], in0=ps_big[0][0:1, 512:1024],
                                                                   scalar=1.0 / 2048.0, in1=rsc[0:1, 512:1024],
                                                                   op0=ALU.mult, op1=ALU.subtract), after=[r2, mm_q[-1]])
            r4 = s.emit("vector", lambda e: e.tensor_scalar(out=rsc[0:1, 512:1024], in0=rsc[0:1, 512:1024], scalar1=1e-5,
                                                            scalar2=None, op0=ALU.add), after=[r3])
            r5 = s.emit("scalar", lambda e: e.activation(out=rsc[0:1, 512:1024], in_=rsc[0:1, 512:1024], func=AF.Ln), after=[r4])
            r6 = s.emit("scalar", lambda e: e.activation(out=rsc[0:1, 512:1024], in_=rsc[0:1, 512:1024], func=AF.Exp,
                                                         scale=-0.5), after=[r5])
            tb1 = s.emit("tensor", lambda e: e.matmul(ps_big[1][:, 0:512], ones1[0:1, :], rsc[0:1, 512:1024],
                                                      start=True, stop=True),
                         after=[r6, t_ones1] + ([ln_prev] if ln_prev else []))
            tb2 = s.emit("vector", lambda e: e.tensor_copy(out=binv[:, 0:512], in_=ps_big[1][:, 0:512]), after=[tb1])
            for hc in range(HC):
                par = hc % 2
                t_l2 = s.emit("sync", lambda e, hc=hc, par=par, tk0=tk0: e.dma_start(
                    out=xa[par][:], in_=resid_r[:, hc, tk0:tk0 + 512]),
                    after=[xa_rd[par]], dma=True, group=f"ax{par}")
                std = s.emit("vector", lambda e, par=par: e.scalar_tensor_tensor(
                    out=scr[:, 0:512], in0=ps_big[0][:, 0:512], scalar=-1.0 / 2048.0,
                    in1=xa[par][:], op0=ALU.mult, op1=ALU.add), after=[t_l2, tb2])
                tw = s.emit("vector", lambda e, hc=hc, tb=tb: e.tensor_tensor(
                    out=xn[:, hc * XSTR + 1 + tb * 512: hc * XSTR + 1 + tb * 512 + 512],
                    in0=scr[:, 0:512], in1=binv[:, 0:512], op=ALU.mult), after=[std, t_z])
                xa_rd[par] = std
                ln_prev = tw
        t_ln = ln_prev

        kv_toks = {}
        pair_planes = {}
        wa_ld = [None, None]
        wa_rd = [None, None]
        evac_done = [None, None]
        scan_chain = []
        oc_order = []
        for j in range(8):
            oc_order += [j, j + 8]
        for idx, oc in enumerate(oc_order):
            par = idx % 2
            t_w = s.emit("gpsimd", lambda e, oc=oc, par=par: e.dma_start(
                out=wa[par][:].rearrange("p (kc c) -> p kc c", kc=2 * HC),
                in_=waT_r[:, :, oc * 128:(oc + 1) * 128]),
                after=[wa_ld[par], wa_rd[par]] + ([] if wa_ld[par] else tail), dma=True, group=f"awa{par}")
            wa_ld[par] = t_w
            mm = []
            for tb in range(4):
                for kc in range(2 * HC):
                    hcs = kc if kc < HC else kc - HC
                    off = 0 if kc < HC else 1
                    mm.append(s.emit("tensor", lambda e, kc=kc, tb=tb, par=par, hcs=hcs, off=off: e.matmul(
                        ps_big[par][:, tb * 512:(tb + 1) * 512],
                        wa[par][:, kc * 128:(kc + 1) * 128],
                        xn[:, hcs * XSTR + off + tb * 512: hcs * XSTR + off + tb * 512 + 512],
                        start=(kc == 0), stop=(kc == 2 * HC - 1)),
                        after=[t_w, t_ln, evac_done[par]] + ([mm[-1]] if mm else [])))
            wa_rd[par] = mm[-1]
            plane = pool[0] if oc < 8 else pool[1]
            ev1 = s.emit("scalar", lambda e, par=par, plane=plane: e.activation(
                out=plane[:], in_=ps_big[par][:], func=AF.Sigmoid),
                after=[mm[-1]] + scan_chain[-1:])
            ev2 = s.emit("vector", lambda e, plane=plane: e.tensor_scalar(
                out=plane[:], in0=plane[:], scalar1=0.5, scalar2=None, op0=ALU.max), after=[ev1])
            evac_done[par] = ev2
            pair_planes[oc] = ev2
            if oc >= 8:
                jj = oc - 8
                A, B, Cp, D, Fm = pool[0], pool[1], pool[2], pool[3], pool[4]
                dep = [pair_planes[jj], pair_planes[oc]] + scan_chain[-1:]
                t1 = s.emit("vector", lambda e: e.tensor_tensor(out=Fm[:], in0=B[:], in1=A[:], op=ALU.is_gt), after=dep)
                t2 = s.emit("scalar", lambda e: e.activation(out=Cp[:], in_=A[:], func=AF.Square), after=dep)
                t3 = s.emit("scalar", lambda e: e.activation(out=D[:], in_=B[:], func=AF.Square), after=[t2])
                t4 = s.emit("vector", lambda e: e.tensor_tensor(out=Cp[:], in0=Cp[:], in1=D[:], op=ALU.add), after=[t1, t3])
                t5 = s.emit("scalar", lambda e: e.activation(out=Cp[:], in_=Cp[:], func=AF.Ln), after=[t4])
                t6 = s.emit("vector", lambda e: e.tensor_tensor_scan(out=D[:], data0=Cp[:], data1=Cp[:],
                                                                     initial=-1e30, op0=ALU.max, op1=ALU.bypass), after=[t5])
                t7 = s.emit("vector", lambda e: e.tensor_tensor(out=D[:], in0=Cp[:], in1=D[:], op=ALU.subtract), after=[t6])
                t8 = s.emit("vector", lambda e: e.tensor_tensor_scan(out=D[:], data0=D[:], data1=D[:],
                                                                     initial=0.0, op0=ALU.add, op1=ALU.bypass), after=[t7])
                t9 = s.emit("scalar", lambda e: e.activation(out=D[:], in_=D[:], func=AF.Exp, scale=0.5), after=[t8])
                t10 = s.emit("vector", lambda e: e.tensor_tensor(out=Cp[:], in0=A[:], in1=B[:], op=ALU.max), after=[t7])
                t11 = s.emit("scalar", lambda e: e.activation(out=Cp[:], in_=Cp[:], func=AF.Ln), after=[t10, t9])
                t12 = s.emit("scalar", lambda e: e.activation(out=Cp[:], in_=Cp[:], func=AF.Exp, scale=-1.0), after=[t11])
                t13 = s.emit("vector", lambda e: e.tensor_tensor(out=B[:], in0=A[:], in1=B[:], op=ALU.min), after=[t10])
                t14 = s.emit("vector", lambda e: e.tensor_tensor(out=Cp[:], in0=B[:], in1=Cp[:], op=ALU.mult), after=[t12, t13])
                t15 = s.emit("scalar", lambda e: e.activation(out=Cp[:], in_=Cp[:], func=AF.Arctan), after=[t14])
                u1 = s.emit("vector", lambda e: e.tensor_tensor(out=A[:], in0=Fm[:], in1=Cp[:], op=ALU.mult), after=[t15])
                u2 = s.emit("vector", lambda e: e.scalar_tensor_tensor(out=A[:], in0=A[:], scalar=-2.0,
                                                                       in1=Cp[:], op0=ALU.mult, op1=ALU.add), after=[u1])
                u3 = s.emit("vector", lambda e: e.scalar_tensor_tensor(out=A[:], in0=Fm[:], scalar=float(np.pi / 2),
                                                                       in1=A[:], op0=ALU.mult, op1=ALU.add), after=[u2])
                t19 = s.emit("vector", lambda e: e.tensor_tensor_scan(out=A[:], data0=A[:], data1=A[:],
                                                                      initial=0.0, op0=ALU.add, op1=ALU.bypass), after=[u3])
                c1 = s.emit("vector", lambda e: e.tensor_scalar(out=Cp[:], in0=A[:], scalar1=INV2PI, scalar2=0.25,
                                                                op0=ALU.mult, op1=ALU.add), after=[t19])
                c2 = s.emit("vector", lambda e: e.tensor_scalar(out=B[:], in0=Cp[:], scalar1=MAGIC, scalar2=None,
                                                                op0=ALU.add), after=[c1])
                c3 = s.emit("vector", lambda e: e.tensor_scalar(out=B[:], in0=B[:], scalar1=MAGIC, scalar2=None,
                                                                op0=ALU.subtract), after=[c2])
                c4 = s.emit("vector", lambda e: e.tensor_tensor(out=Cp[:], in0=Cp[:], in1=B[:], op=ALU.subtract), after=[c3])
                c5 = s.emit("scalar", lambda e: e.activation(out=Cp[:], in_=Cp[:], func=AF.Sin, scale=TWO_PI), after=[c4])
                s1 = s.emit("vector", lambda e: e.tensor_scalar(out=B[:], in0=A[:], scalar1=INV2PI, scalar2=MAGIC,
                                                                op0=ALU.mult, op1=ALU.add), after=[c4])
                s2 = s.emit("vector", lambda e: e.tensor_scalar(out=B[:], in0=B[:], scalar1=MAGIC, scalar2=None,
                                                                op0=ALU.subtract), after=[s1])
                s3 = s.emit("vector", lambda e: e.scalar_tensor_tensor(out=B[:], in0=A[:], scalar=INV2PI,
                                                                       in1=B[:], op0=ALU.mult, op1=ALU.subtract), after=[s2])
                s4 = s.emit("scalar", lambda e: e.activation(out=B[:], in_=B[:], func=AF.Sin, scale=TWO_PI), after=[s3, c5])
                k1 = s.emit("vector", lambda e, jj=jj: e.tensor_tensor(out=kv[:, jj * 2048:(jj + 1) * 2048],
                                                                       in0=D[:], in1=Cp[:], op=ALU.mult), after=[s4, t9])
                k2 = s.emit("vector", lambda e, jj=jj: e.tensor_tensor(out=kv[:, (jj + 8) * 2048:(jj + 9) * 2048],
                                                                       in0=D[:], in1=B[:], op=ALU.mult), after=[k1])
                kv_toks[jj] = k2
                scan_chain.append(k2)

        t_kv = scan_chain[-1]

        wv_ld = [None, None]
        wr_ld = [None, None]
        val_rd = [t_kv, t_kv]
        rec_rd = [t_kv, t_kv]
        ev_done = [None, None]
        st_toks = []
        xa2_rd = [None, None]
        for oc in range(HC):
            par = oc % 2
            t_wv = s.emit("gpsimd", lambda e, oc=oc, par=par: e.dma_start(
                out=wv[par][:].rearrange("p (kc c) -> p kc c", kc=HC),
                in_=wvT_r[:, :, oc * 128:(oc + 1) * 128]),
                after=[wv_ld[par] or t_kv, val_rd[par]], dma=True, group=f"awv{par}")
            wv_ld[par] = t_wv
            mmv = []
            for tb in range(4):
                for kc in range(HC):
                    mmv.append(s.emit("tensor", lambda e, kc=kc, tb=tb, par=par: e.matmul(
                        ps_big[par][:, tb * 512:(tb + 1) * 512],
                        wv[par][:, kc * 128:(kc + 1) * 128],
                        kv[:, kc * 2048 + tb * 512: kc * 2048 + tb * 512 + 512],
                        start=(kc == 0), stop=(kc == HC - 1)),
                        after=[t_wv, t_kv, ev_done[par]] + ([mmv[-1]] if mmv else [])))
            val_rd[par] = mmv[-1]
            t_wr = s.emit("gpsimd", lambda e, oc=oc, par=par: e.dma_start(
                out=wa[par][:].rearrange("p (kc c) -> p kc c", kc=2 * HC),
                in_=wrT_r[:, :, oc * 128:(oc + 1) * 128]),
                after=[wr_ld[par] or t_kv, rec_rd[par] if wr_ld[par] else wa_rd[par]], dma=True, group=f"awa{par}")
            wr_ld[par] = t_wr
            evv = s.emit("vector", lambda e, par=par: e.tensor_copy(out=pool[3][:], in_=ps_big[par][:]),
                         after=[mmv[-1]])
            mmr = []
            for tb in range(4):
                for kc in range(2 * HC):
                    hcs = kc if kc < HC else kc - HC
                    off = 0 if kc < HC else 1
                    mmr.append(s.emit("tensor", lambda e, kc=kc, tb=tb, par=par, hcs=hcs, off=off: e.matmul(
                        ps_big[par][:, tb * 512:(tb + 1) * 512],
                        wa[par][:, kc * 128:(kc + 1) * 128],
                        xn[:, hcs * XSTR + off + tb * 512: hcs * XSTR + off + tb * 512 + 512],
                        start=(kc == 0), stop=(kc == 2 * HC - 1)),
                        after=[t_wr, evv] + ([mmr[-1]] if mmr else [])))
            rec_rd[par] = mmr[-1]
            er1 = s.emit("scalar", lambda e, par=par: e.activation(out=pool[4][:], in_=ps_big[par][:], func=AF.Sigmoid),
                         after=[mmr[-1]])
            er2 = s.emit("vector", lambda e: e.tensor_scalar(out=pool[4][:], in0=pool[4][:], scalar1=0.5,
                                                             scalar2=None, op0=ALU.max), after=[er1])
            ev_done[par] = er2
            er3 = s.emit("vector", lambda e: e.tensor_tensor(out=pool[4][:], in0=pool[4][:], in1=pool[4][:],
                                                             op=ALU.mult), after=[er2])
            er4 = s.emit("vector", lambda e: e.tensor_tensor(out=pool[4][:], in0=pool[4][:], in1=pool[4][:],
                                                             op=ALU.mult), after=[er3])
            er5 = s.emit("vector", lambda e: e.tensor_tensor(out=pool[3][:], in0=pool[3][:], in1=pool[4][:],
                                                             op=ALU.mult), after=[er4])
            for tb in range(4):
                sp = tb % 2
                t_x1 = s.emit("sync", lambda e, oc=oc, tb=tb, sp=sp: e.dma_start(
                    out=xa[sp][:], in_=resid_r[:, oc, tb * 512:(tb + 1) * 512]),
                    after=[er5, xa2_rd[sp] if xa2_rd[sp] is not None else xa_rd[sp]], dma=True, group=f"ax{sp}")
                ta = s.emit("vector", lambda e, tb=tb, sp=sp: e.tensor_tensor(
                    out=yst[sp][:], in0=pool[3][:, tb * 512:(tb + 1) * 512], in1=xa[sp][:], op=ALU.add),
                    after=[t_x1] + ([st_toks[-2]] if len(st_toks) >= 2 else []))
                st = s.emit("scalar", lambda e, oc=oc, tb=tb, sp=sp: e.dma_start(
                    out=resid_r[:, oc, tb * 512:(tb + 1) * 512], in_=yst[sp][:]), after=[ta], dma=True, force=True, group="as")
                xa2_rd[sp] = ta
                st_toks.append(st)
        ph.close()
        return [st_toks[-1]]

    tail = list(init_toks)
    if "ffn1" in phases:
        tail = ffn(xT_r, resid_r if ("attn" in phases or "ffn2" in phases) else outT_r,
                   prev_tail=tail, tag="f1")
    qstack.close()
    if "attn" in phases:
        tail = attention(tail)
    if "ffn2" in phases:
        tail = ffn(resid_r, outT_r, prev_tail=tail, tag="f2")

    s.build(nc, exit_engine="sync", exit_waits=tuple(t for t in tail if t is not None))
    gstack.close()
    return nc


_PERM = np.concatenate([np.arange(0, C, 2), np.arange(1, C, 2)])


def _prep_weights(L0_attn_W_act, L0_attn_W_val, L0_attn_W_rec, L0_ffn_w_gate,
                  L0_ffn_w_up, L0_ffn_w_down, L0_ffn_down_reorder_index):
    wa = np.ascontiguousarray(np.asarray(L0_attn_W_act, dtype=np.float32)[_PERM, :].T)
    wr = np.ascontiguousarray(np.asarray(L0_attn_W_rec, dtype=np.float32).T)
    wv = np.ascontiguousarray(np.asarray(L0_attn_W_val, dtype=np.float32)[:, _PERM].T)
    wg = np.ascontiguousarray(np.asarray(L0_ffn_w_gate, dtype=np.float32).T)
    wu = np.ascontiguousarray(np.asarray(L0_ffn_w_up, dtype=np.float32).T)
    wd = np.asarray(L0_ffn_w_down, dtype=np.float32)
    idx = np.asarray(L0_ffn_down_reorder_index)
    wde = np.empty_like(wd)
    wde[:, idx] = wd
    wdT = np.ascontiguousarray(wde.T)
    return wa, wr, wv, wg, wu, wdT


def kernel(L0_attn_x, L0_attn_W_act, L0_attn_W_val, L0_attn_W_rec,
           L0_ffn_w_gate, L0_ffn_w_up, L0_ffn_w_down,
           L0_ffn_up_reorder_index, L0_ffn_down_reorder_index):
    x = np.asarray(L0_attn_x, dtype=np.float32)
    B = x.shape[0]
    wa, wr, wv, wg, wu, wdT = _prep_weights(L0_attn_W_act, L0_attn_W_val, L0_attn_W_rec,
                                            L0_ffn_w_gate, L0_ffn_w_up, L0_ffn_w_down,
                                            L0_ffn_down_reorder_index)
    ident = np.eye(128, dtype=np.float32)
    nc = build_program(PHASES)
    in_maps = []
    for b in range(B):
        in_maps.append({
            "xT": np.ascontiguousarray(x[b].T),
            "waT": wa, "wrT": wr, "wvT": wv, "wgT": wg, "wuT": wu, "wdT": wdT,
            "ident": ident,
        })
    res = run_bass_kernel_spmd(nc, in_maps, core_ids=list(range(B)))
    out = np.stack([np.ascontiguousarray(r["outT"].T) for r in res.results], axis=0)
    return out
